# version 05 (final)
# baseline (speedup 1.0000x reference)
import numpy as np
import ml_dtypes
import concourse.bass as bass
import concourse.mybir as mybir
from concourse.bass_utils import run_bass_kernel_spmd

F32 = mybir.dt.float32
BF16 = mybir.dt.bfloat16
AF = mybir.ActivationFunctionType
OP = mybir.AluOpType
AX = mybir.AxisListType
bfdt = ml_dtypes.bfloat16

NC = 8
B, L, H = 4, 2048, 2048
D, DH = 4096, 2048
SS, RR = 16, 128
II = 8192
T = 1024
NF = 16
NDT = 16
MAGIC = float(np.float32(1.5 * 2.0 ** 23))
INV_H = 1.0 / float(H)
ISQ_I = 1.0 / float(np.sqrt(np.float32(II)))
EPS_LN = 1e-5
EPS_Q = 1e-5
STAGE = 3
MSUB = 9
ARMN = 128

ENGS = ("tensor", "vector", "scalar", "sync", "gpsimd")


class Ctx:

    def __init__(self):
        self.q = {e: [] for e in ENGS}
        self.cnt = {}
        self.wm = {e: {} for e in ENGS}
        self.streams = {}
        self.semnames = []
        for nm in ("t", "v", "s"):
            self.cnt[nm] = 0
            self.semnames.append(nm)

    def stream(self, name, eng, width=2):
        sl = []
        for j in range(width):
            sn = f"{name}{j}"
            self.cnt[sn] = 0
            self.semnames.append(sn)
            sl.append(sn)
        self.streams[name] = [eng, sl, 0]

    def csem(self, name):
        self.cnt[name] = 0
        self.semnames.append(name)

    def emit(self, eng, fn, w=(), inc=None, k=1):
        wl = []
        flat = []
        for tokn in w:
            if tokn is None:
                continue
            if isinstance(tokn, list):
                flat.extend(t for t in tokn if t is not None)
            else:
                flat.append(tokn)
        for tokn in flat:
            sname, vv = tokn
            if vv <= 0:
                continue
            if self.wm[eng].get(sname, 0) < vv:
                self.wm[eng][sname] = vv
                wl.append((sname, vv))
        tok = None
        if inc is not None:
            self.cnt[inc] += k
            tok = (inc, self.cnt[inc])
        self.q[eng].append((fn, wl, (inc, k) if inc else None))
        return tok

    def dma(self, stream, fn, w=()):
        eng, sl, idx = self.streams[stream]
        sn = sl[idx % len(sl)]
        self.streams[stream][2] += 1
        wl = list(w) + [(sn, self.cnt[sn])]
        return self.emit(eng, fn, wl, inc=sn, k=16)

    def fence(self, stream):
        sl = self.streams[stream][1]
        return [(sn, self.cnt[sn]) for sn in sl]

    def t(self, fn, w=()):
        return self.emit("tensor", fn, w, "t")

    def tq(self, fn, w=()):
        return self.emit("tensor", fn, w, None)

    def v(self, fn, w=()):
        return self.emit("vector", fn, w, "v")

    def s(self, fn, w=()):
        return self.emit("scalar", fn, w, "s")

    def tok(self, sem):
        return (sem, self.cnt[sem])


def build_program(stage=STAGE, msub=None):
    if msub is None:
        msub = MSUB
    nc = bass.Bass("TRN2", target_bir_lowering=False, debug=False)

    d_x0 = nc.dram_tensor("x0", [T, H], F32, kind="ExternalInput")
    d_out = nc.dram_tensor("xout", [T, H], F32, kind="ExternalOutput")
    d_wg = nc.dram_tensor("wgb", [128, 64 * 2048], F32, kind="ExternalInput")
    d_wu = nc.dram_tensor("wub", [128, 64 * 2048], F32, kind="ExternalInput")
    d_wd = nc.dram_tensor("wdb", [128, 16 * 8192], F32, kind="ExternalInput")
    d_win = nc.dram_tensor("winb", [128, 32 * 2048], F32, kind="ExternalInput")
    d_wout = nc.dram_tensor("woutb", [128, 16 * 2048], F32, kind="ExternalInput")
    d_wx = nc.dram_tensor("wxb", [128, 16 * 160], F32, kind="ExternalInput")
    d_wdt = nc.dram_tensor("wdtb", [128, DH], F32, kind="ExternalInput")
    d_cw = nc.dram_tensor("cwt", [128, 3 * NDT], F32, kind="ExternalInput")
    d_cb = nc.dram_tensor("cbt", [128, NDT], F32, kind="ExternalInput")
    d_dd = nc.dram_tensor("ddt", [128, NDT], F32, kind="ExternalInput")
    d_bdt = nc.dram_tensor("bdtt", [128, NDT], F32, kind="ExternalInput")
    d_al = nc.dram_tensor("alt", [128, SS * NDT], F32, kind="ExternalInput")
    d_id = nc.dram_tensor("ident", [128, 128], F32, kind="ExternalInput")
    d_h128 = nc.dram_tensor("h128", [128, 128], BF16, kind="ExternalInput")
    d_ones = nc.dram_tensor("onesc", [128, 2], F32, kind="ExternalInput")
    d_onesb = nc.dram_tensor("onesb", [128, 2], BF16, kind="ExternalInput")
    d_onerf = nc.dram_tensor("onerowf", [1, 128], F32, kind="ExternalInput")
    d_eall = nc.dram_tensor("eall", [32, 32 * 128], BF16, kind="ExternalInput")
    d_p1 = nc.dram_tensor("p1chunk", [128, 3 * 16384], F32, kind="ExternalInput")

    p_x = nc.dram_tensor("prk_x", [128, NF * T], F32)
    p_x3 = nc.dram_tensor("prk_x3", [128, NF * T], F32)
    p_z = nc.dram_tensor("prk_z", [128, NDT * L], BF16)
    p_xc = nc.dram_tensor("prk_xc", [128, NDT * L], BF16)
    p_dt = nc.dram_tensor("prk_dt", [128, NDT * L], BF16)
    p_po = nc.dram_tensor("prk_po", [128, NF * L], F32)
    wq_g = nc.dram_tensor("wq_g", [128, 64 * 2048], BF16)
    wq_u = nc.dram_tensor("wq_u", [128, 64 * 2048], BF16)
    wq_d = nc.dram_tensor("wq_d", [128, 16 * 8192], BF16)
    cc_mi = nc.dram_tensor("cc_mi", [1, 8], F32)
    cc_mo = nc.dram_tensor("cc_mo", [1, 8], F32)
    NBAR = 8
    cc_bi = [nc.dram_tensor(f"cc_bi{i}", [1, 8], F32) for i in range(NBAR)]
    cc_bo = [nc.dram_tensor(f"cc_bo{i}", [1, 8], F32) for i in range(NBAR)]

    s_xln = nc.dram_tensor("shr_xln", [2, 128, NF * T], BF16, addr_space="Shared")
    s_xdbl = nc.dram_tensor("shr_xdbl", [2, 160, L], F32, addr_space="Shared")
    s_po = nc.dram_tensor("shr_po", [2, 128, NF * T], F32, addr_space="Shared")

    po_v = p_po.ap().rearrange("c (m t) -> c m t", m=NF)

    off = [0]
    MBASE = 16512
    nc.alloc_sbuf_tensor("arena_rsv", [128, 53008], F32)

    def salloc(name, shape, dtype, at=None):
        a = off[0] if at is None else at
        h = nc.alloc_sbuf_tensor_at(name, shape, dtype, offset=MBASE + a)
        if at is None:
            sz = int(np.prod(shape[1:])) * mybir.dt.size(dtype)
            off[0] = a + ((sz + 31) // 32) * 32
        return h

    id128 = salloc("id128", [128, 128], F32)
    h128 = salloc("h128s", [128, 128], BF16)
    ones_c = salloc("ones_c", [128, 2], F32)
    ones_cb = salloc("ones_cb", [128, 2], BF16)
    ones_rf = salloc("ones_rf", [1, 128], F32)
    eall = salloc("ealls", [32, 32 * 128], BF16)
    cwt = salloc("cwts", [128, 3 * NDT], F32)
    cbt = salloc("cbts", [128, NDT], F32)
    ddt = salloc("ddts", [128, NDT], F32)
    bdtt = salloc("bdtts", [128, NDT], F32)
    alt = salloc("alts", [128, SS * NDT], F32)
    acol = salloc("acols", [128, SS * NDT], F32)
    wsc = salloc("wscs", [128, 8], F32)
    colb = salloc("colb", [128, 16], F32)
    states = salloc("states", [128, NDT * SS], F32)
    ccsb = salloc("ccsb", [1, 16], F32)
    assert off[0] <= 13280, off[0]
    spare_t = salloc("spare_t", [128, 1024], BF16, at=13280)

    AXL = 15360
    AXH = AXL + 32 * 1024
    AB0 = AXH + 32 * 1024
    assert MBASE + AB0 + 128 * 1024 <= 228500
    x_lo = salloc("x_lo", [128, 8 * T], F32, at=AXL)
    xq_t = salloc("xq_t", [128, 16 * T], BF16, at=AXL)
    qtf = salloc("qtf", [128, 2048], F32, at=AXL)
    mbb = salloc("mbb", [128, 1024], BF16, at=AXL + 8192)
    mff = salloc("mff", [128, 1024], F32, at=AXL + 10240)
    wdh = salloc("wdh", [128, 16384], BF16, at=AXL)
    winb = salloc("winb", [128, 4096], BF16, at=AXL)
    xpscr = salloc("xpscr", [128, 2048], F32, at=AXL + 8192)
    cvscr = salloc("cvscr", [128, 2048], F32, at=AXL + 16384)
    xcstg = salloc("xcstg", [128, 4096], BF16, at=AXL + 24576)
    xcb = salloc("xcb", [128, 4096], BF16, at=AXL)
    xdbla = salloc("xdbla", [128, 2048], F32, at=AXL + 8192)
    dtinb = salloc("dtinb", [128, 2048], BF16, at=AXL + 16384)
    t32f = salloc("t32f", [32, 2048], F32, at=AXL + 20480)
    wdtb_s = salloc("wdtb_s", [128, 2048], BF16, at=AXL + 28672)
    das = salloc("das", [128, 2048], F32, at=AXL)
    wsc_ = salloc("wsc_", [128, 1024], BF16, at=AXL + 8192)
    hsc = salloc("hsc", [128, 1024], BF16, at=AXL + 10240)
    p0s = salloc("p0s", [128, 1024], BF16, at=AXL + 12288)
    p1s = salloc("p1s", [128, 1024], BF16, at=AXL + 14336)
    bbcb = salloc("bbcb", [128, 4096], BF16, at=AXL + 16384)
    zst = salloc("zst", [128, 2048], BF16, at=AXL + 24576)
    rowb = salloc("rowb", [1, 8 * T], F32, at=AXH)
    wgu = salloc("wgu", [128, 8192], BF16, at=AXH)
    vgu = salloc("vgu", [128, 6144], BF16, at=AXH + 16384)
    xst = salloc("xst", [128, 2048], F32, at=AXH)
    est = salloc("est", [128, 1024], F32, at=AXH + 8192)
    scomb = salloc("scomb", [128, 1024], F32, at=AXH + 12288)
    zstg = salloc("zstg", [128, 2048], BF16, at=AXH)
    wxs = salloc("wxs", [128, 2560], BF16, at=AXH + 4096)
    dtstg = salloc("dtstg", [128, 4096], BF16, at=AXH + 9216)
    woutb_s = salloc("woutb_s", [128, 4096], BF16, at=AXH)
    postg = salloc("postg", [128, 2048], F32, at=AXH + 8192)
    bc_bf = salloc("bc_bf", [32, 2048], BF16, at=AXH + 17664)
    arb_b = salloc("arb_b", [128, 64 * T], BF16, at=AB0)
    arb_f = salloc("arb_f", [128, 32 * T], F32, at=AB0)
    p2st = salloc("p2st", [128, 4096], F32, at=AB0)
    p1st = salloc("p1st", [128, 4352], F32, at=AB0 + 16384)
    p2wq = salloc("p2wq", [128, 4096], BF16, at=AB0 + 33792)
    lncast = salloc("lncast", [128, 4096], BF16, at=AB0 + 41984)
    mtt2 = salloc("mtt2", [128, 1024], F32, at=AB0 + 41984)
    dff = salloc("dff", [128, 2048], F32, at=AB0 + 50176)
    mtt = salloc("mtt", [128, 1024], F32, at=AB0 + 58368)
    mlncast = salloc("mlncast", [128, 4096], BF16, at=AB0 + 65536)
    mdff = salloc("mdff", [128, 2048], F32, at=AB0 + 73728)
    x_hi = salloc("x_hi", [128, 8 * T], F32, at=AB0 + 98304)

    def xtile(k):
        if k < 8:
            return x_lo[:, k * T:(k + 1) * T]
        return x_hi[:, (k - 8) * T:(k - 7) * T]

    ctx = Ctx()
    for nm, eng, wd in (
        ("cst", "sync", 1), ("xin", "sync", 1), ("prkw", "sync", 2),
        ("prkr", "sync", 2), ("cach", "sync", 2), ("xcp", "sync", 2),
        ("zp", "sync", 2), ("dtp", "sync", 2), ("xcbr", "sync", 2),
        ("sdt", "sync", 2), ("sxc", "sync", 2), ("zl", "sync", 2),
        ("pop", "sync", 2), ("e1w", "sync", 1), ("e1r", "sync", 1),
        ("e2w", "sync", 1), ("e3w", "sync", 1), ("outl", "sync", 2),
        ("outs", "sync", 2),
        ("p1l", "scalar", 2), ("p2l", "scalar", 2), ("wgl", "scalar", 2),
        ("wdn", "scalar", 4),
        ("winl", "gpsimd", 2), ("wxl", "gpsimd", 1), ("wdtl", "gpsimd", 1),
        ("wol", "gpsimd", 2), ("barw", "gpsimd", 1), ("miw", "gpsimd", 1),
        ("mir", "gpsimd", 1), ("e2r", "gpsimd", 1), ("e3r", "gpsimd", 1),
    ):
        ctx.stream(nm, eng, wd)
    for i in range(9):
        ctx.csem(f"ccb{i}")

    import contextlib
    with (
        nc.psum_tensor("ps0", [128, 1024], F32) as ps0,
        nc.psum_tensor("ps1", [128, 1024], F32) as ps1,
        nc.psum_tensor("ps2", [128, 1024], F32) as ps2,
        nc.psum_tensor("ps3", [128, 1024], F32) as ps3,
        contextlib.ExitStack() as _stk,
    ):
        PS = [ps0, ps1, ps2, ps3]
        sems = {}
        bar_i = [0]

        def barrier(w=(), pairs=True):
            i = bar_i[0]
            bar_i[0] += 1
            assert i < NBAR
            rg = [[0, 1], [2, 3], [4, 5], [6, 7]] if pairs else [list(range(NC))]
            bw = ctx.dma("barw", lambda e, S, i=i: e.dma_start(
                out=cc_bi[i][:, :], in_=ccsb[0:1, 0:8]),
                w=list(w) + [ctx.tok("v")])
            return ctx.emit("gpsimd", lambda e, S, i=i, rg=rg:
                            e.collective_compute(
                "AllReduce", OP.add, replica_groups=rg,
                ins=[cc_bi[i].ap().opt()], outs=[cc_bo[i].ap().opt()]),
                w=[bw], inc=f"ccb{i}", k=1)

        r_mu, r_e2, r_sd, r_am, r_rs, r_mrs, r_dq, r_t = [
            (i * T, (i + 1) * T) for i in range(8)]

        def row(rr):
            return rowb[0:1, rr[0]:rr[1]]

        ldc = []
        for dst, src in ((id128, d_id), (h128, d_h128), (ones_c, d_ones),
                         (ones_cb, d_onesb), (ones_rf, d_onerf), (eall, d_eall),
                         (cwt, d_cw), (cbt, d_cb), (ddt, d_dd), (bdtt, d_bdt),
                         (alt, d_al)):
            ldc.append(ctx.dma("cst", lambda e, S, dst=dst, src=src:
                                e.dma_start(out=dst[:, :], in_=src[:, :])))
        tk = ctx.s(lambda e, S: e.activation(out=acol[:, :], in_=alt[:, :],
                                             func=AF.Exp, bias=0.0, scale=1.0),
                   w=[ldc[-1]])
        ctx.v(lambda e, S: e.tensor_scalar(out=acol[:, :], in0=acol[:, :],
                                           scalar1=-1.0, scalar2=None,
                                           op0=OP.mult), w=[tk])
        ctx.v(lambda e, S: e.memset(ccsb[:, :], 0.0))
        ctx.v(lambda e, S: e.memset(states[:, :], 0.0))

        ldx = []
        for tc in range(8):
            ldx.append(ctx.dma("xin", lambda e, S, tc=tc: e.dma_start(
                out=arb_f[:, tc * 2048:(tc + 1) * 2048],
                in_=d_x0[tc * 128:(tc + 1) * 128, :])))
        cp = []
        for tc in range(8):
            for fc in range(NF):
                k = tc * NF + fc
                pw = [ldc[0], ldx[tc]] + ([cp[k - 2]] if k >= 2 else [])
                tt = ctx.t(lambda e, S, tc=tc, fc=fc, k=k: e.transpose(
                    out=PS[k % 2][:, 0:128],
                    in_=arb_f[:, tc * 2048 + fc * 128: tc * 2048 + fc * 128 + 128],
                    identity=id128[:, :]), w=pw)
                cp.append(ctx.v(lambda e, S, tc=tc, fc=fc, k=k: e.tensor_copy(
                    out=xtile(fc)[:, tc * 128: tc * 128 + 128],
                    in_=PS[k % 2][:, 0:128]), w=[tt]))
        tok_x_in = cp[-1]

        a0 = ctx.v(lambda e, S: e.memset(p1st[:, 4096:4104], 0.0),
                   w=[tok_x_in])
        prev_r = a0
        ld_sl = [None, None]
        for ci in range(24):
            mi = ci // 8
            sl = ci % 2
            fo = sl * 2048
            pw = [tok_x_in] + ([ld_sl[sl]] if ld_sl[sl] else [])
            ld = ctx.dma("p1l", lambda e, S, ci=ci, fo=fo: e.dma_start(
                out=p1st[:, fo:fo + 2048],
                in_=d_p1[:, ci * 2048:(ci + 1) * 2048]), w=pw)
            rd = ctx.v(lambda e, S, fo=fo: e.tensor_reduce(
                out=p1st[:, 4100:4101], in_=p1st[:, fo:fo + 2048], axis=AX.X,
                op=OP.add, apply_absolute_value=True), w=[ld, prev_r])
            prev_r = ctx.v(lambda e, S, mi=mi: e.tensor_tensor(
                out=p1st[:, 4096 + mi:4097 + mi],
                in0=p1st[:, 4096 + mi:4097 + mi],
                in1=p1st[:, 4100:4101], op=OP.add), w=[rd])
            ld_sl[sl] = rd
        tsum = ctx.t(lambda e, S: e.matmul(
            PS[2][0:1, 0:3], ones_c[:, 0:1], p1st[:, 4096:4099],
            start=True, stop=True), w=[prev_r, tok_x_in])
        cps = ctx.v(lambda e, S: e.tensor_copy(out=ccsb[0:1, 0:3],
                                               in_=PS[2][0:1, 0:3]), w=[tsum])
        miw = ctx.dma("miw", lambda e, S: e.dma_start(
            out=cc_mi[:, :], in_=ccsb[0:1, 0:8]), w=[cps])
        cct = ctx.emit("gpsimd", lambda e, S: e.collective_compute(
            "AllReduce", OP.add, replica_groups=[list(range(NC))],
            ins=[cc_mi.ap().opt()], outs=[cc_mo.ap().opt()]), w=[miw],
            inc="ccb8", k=1)
        ldmo = ctx.dma("mir", lambda e, S: e.dma_start(
            out=ccsb[0:1, 8:16], in_=cc_mo[:, :]), w=[cct])
        NW = 1.0 / float(II) / float(H)
        tk = ctx.v(lambda e, S: e.tensor_scalar(
            out=ccsb[0:1, 8:11], in0=ccsb[0:1, 8:11], scalar1=NW,
            scalar2=EPS_Q, op0=OP.mult, op1=OP.max), w=[ldmo])
        tk = ctx.v(lambda e, S: e.reciprocal(out=ccsb[0:1, 12:15],
                                             in_=ccsb[0:1, 8:11]), w=[tk])
        tb = ctx.t(lambda e, S: e.matmul(PS[2][:, 0:8], ones_rf[0:1, :],
                                         ccsb[0:1, 8:16], start=True,
                                         stop=True), w=[tk, cps])
        tkw = ctx.v(lambda e, S: e.tensor_copy(out=wsc[:, 0:3],
                                               in_=PS[2][:, 4:7]), w=[tb])
        tok_wsc = ctx.v(lambda e, S: e.tensor_copy(out=wsc[:, 3:6],
                                                   in_=PS[2][:, 0:3]),
                        w=[tb, tkw])

        st_tok = [None, None]
        wq_tok = [None, None]
        chunks = [(0, d_wg, wq_g, b) for b in range(64)] + \
                 [(1, d_wu, wq_u, b) for b in range(64)] + \
                 [(2, d_wd, wq_d, b) for b in range(64)]
        for ci, (mi, src, dst, b) in enumerate(chunks):
            sl = ci % 2
            fo = sl * 2048
            bo = sl * 2048
            pw = [tok_wsc] + ([st_tok[sl]] if st_tok[sl] else [])
            ld = ctx.dma("p2l", lambda e, S, src=src, b=b, fo=fo:
                         e.dma_start(
                out=p2st[:, fo:fo + 2048],
                in_=src[:, b * 2048:(b + 1) * 2048]), w=pw)
            q1 = ctx.v(lambda e, S, fo=fo, mi=mi: e.tensor_scalar(
                out=p2st[:, fo:fo + 2048], in0=p2st[:, fo:fo + 2048],
                scalar1=wsc[:, mi:mi + 1], scalar2=MAGIC, op0=OP.mult,
                op1=OP.add), w=[ld])
            q2 = ctx.v(lambda e, S, fo=fo: e.tensor_scalar(
                out=p2st[:, fo:fo + 2048], in0=p2st[:, fo:fo + 2048],
                scalar1=MAGIC, scalar2=1.0, op0=OP.subtract, op1=OP.min),
                w=[q1])
            pw3 = [q2] + ([wq_tok[sl]] if wq_tok[sl] else [])
            q3 = ctx.v(lambda e, S, fo=fo, bo=bo: e.tensor_scalar(
                out=p2wq[:, bo:bo + 2048], in0=p2st[:, fo:fo + 2048],
                scalar1=-1.0, scalar2=None, op0=OP.max), w=pw3)
            st_tok[sl] = q3
            wq_tok[sl] = ctx.dma("cach", lambda e, S, dst=dst, b=b, bo=bo:
                                 e.dma_start(
                out=dst[:, b * 2048:(b + 1) * 2048],
                in_=p2wq[:, bo:bo + 2048]), w=[q3])
        tok_cache = ctx.fence("cach")

        def ln_stats(w_entry, cst):
            stat = []
            for k in range(NF):
                c = ctx.v(lambda e, S, k=k: e.tensor_copy(
                    out=cst[:, (k % 2) * 1024:(k % 2) * 1024 + 1024],
                    in_=xtile(k)),
                    w=[w_entry] + ([stat[2 * k - 4]] if k >= 2 else []))
                sq = ctx.s(lambda e, S, k=k: e.activation(
                    out=cst[:, 2048 + (k % 2) * 1024:
                            2048 + (k % 2) * 1024 + 1024],
                    in_=xtile(k), func=AF.Square,
                    bias=0.0, scale=1.0),
                    w=[w_entry] + ([stat[2 * k - 3]] if k >= 2 else []))
                m1 = m2 = None
                for h2 in range(2):
                    m1 = ctx.t(lambda e, S, k=k, h2=h2: e.matmul(
                        PS[0][0:1, h2 * 512:h2 * 512 + 512], ones_cb[:, 0:1],
                        cst[:, (k % 2) * 1024 + h2 * 512:
                            (k % 2) * 1024 + h2 * 512 + 512],
                        start=(k == 0), stop=(k == NF - 1)), w=[c])
                    m2 = ctx.t(lambda e, S, k=k, h2=h2: e.matmul(
                        PS[1][0:1, h2 * 512:h2 * 512 + 512], ones_cb[:, 0:1],
                        cst[:, 2048 + (k % 2) * 1024 + h2 * 512:
                            2048 + (k % 2) * 1024 + h2 * 512 + 512],
                        start=(k == 0), stop=(k == NF - 1)), w=[sq])
                stat.extend([m1, m2])
            tstat = stat[-1]
            tk = ctx.v(lambda e, S: e.tensor_scalar(
                out=row(r_mu), in0=PS[0][0:1, :], scalar1=INV_H, scalar2=None,
                op0=OP.mult), w=[tstat])
            tk = ctx.v(lambda e, S: e.tensor_scalar(
                out=row(r_e2), in0=PS[1][0:1, :], scalar1=INV_H, scalar2=None,
                op0=OP.mult), w=[tstat, tk])
            tk = ctx.v(lambda e, S: e.tensor_tensor(
                out=row(r_t), in0=row(r_mu), in1=row(r_mu), op=OP.mult),
                w=[tk])
            tk = ctx.v(lambda e, S: e.tensor_tensor(
                out=row(r_e2), in0=row(r_e2), in1=row(r_t), op=OP.subtract),
                w=[tk])
            tk = ctx.v(lambda e, S: e.tensor_scalar(
                out=row(r_e2), in0=row(r_e2), scalar1=EPS_LN, scalar2=None,
                op0=OP.add), w=[tk])
            ts_ = ctx.s(lambda e, S: e.activation(
                out=row(r_sd), in_=row(r_e2), func=AF.Sqrt, bias=0.0,
                scale=1.0), w=[tk])
            tk = ctx.v(lambda e, S: e.reciprocal(out=row(r_sd),
                                                 in_=row(r_sd)), w=[ts_])
            tk = ctx.v(lambda e, S: e.tensor_tensor(
                out=row(r_t), in0=row(r_sd), in1=row(r_sd), op=OP.mult),
                w=[tk])
            tk = ctx.v(lambda e, S: e.tensor_tensor(
                out=row(r_t), in0=row(r_t), in1=row(r_e2), op=OP.mult),
                w=[tk])
            tk = ctx.v(lambda e, S: e.tensor_scalar(
                out=row(r_t), in0=row(r_t), scalar1=-0.5, scalar2=1.5,
                op0=OP.mult, op1=OP.add), w=[tk])
            tok_rows = ctx.v(lambda e, S: e.tensor_tensor(
                out=row(r_sd), in0=row(r_sd), in1=row(r_t), op=OP.mult),
                w=[tk])
            return tok_rows, tstat

        def bcast(ps_i, rr, w):
            tb_ = None
            for h2 in range(2):
                tb_ = ctx.t(lambda e, S, h2=h2, ps_i=ps_i, rr=rr: e.matmul(
                    PS[ps_i][:, h2 * 512:h2 * 512 + 512], ones_rf[0:1, :],
                    rowb[0:1, rr[0] + h2 * 512: rr[0] + h2 * 512 + 512],
                    start=True, stop=True), w=w)
            return tb_

        def amax_to_row(src_fn, w):
            prev = None
            for tc8 in range(8):
                tt = ctx.t(lambda e, S, tc8=tc8: e.transpose(
                    out=PS[3][:, 0:128], in_=src_fn(tc8),
                    identity=id128[:, :]), w=list(w) + ([prev] if prev else []))
                rv = ctx.v(lambda e, S, tc8=tc8: e.tensor_reduce(
                    out=colb[:, tc8:tc8 + 1], in_=PS[3][:, 0:128], axis=AX.X,
                    op=OP.max), w=[tt])
                tt2 = ctx.t(lambda e, S, tc8=tc8: e.transpose(
                    out=PS[3][0:1, 128:256], in_=colb[:, tc8:tc8 + 1],
                    identity=id128[:, :]), w=[rv])
                prev = ctx.v(lambda e, S, tc8=tc8: e.tensor_copy(
                    out=rowb[0:1, r_am[0] + tc8 * 128:
                             r_am[0] + tc8 * 128 + 128],
                    in_=PS[3][0:1, 128:256]), w=[tt2])
            return prev

        def emit_ffn(dst_park, w_entry):
            tok_rows, tstat = ln_stats(w_entry, lncast)
            tmu = bcast(2, r_mu, [tok_rows, tstat])
            df = [dff[:, 0:1024], dff[:, 1024:2048]]
            mt = mtt[:, 0:1024]
            mtn = mtt2[:, 0:1024]
            prev = None
            for k in range(NF):
                d = df[k % 2]
                t1 = ctx.v(lambda e, S, k=k, d=d: e.tensor_tensor(
                    out=d, in0=xtile(k), in1=PS[2][:, :],
                    op=OP.subtract), w=[tmu] + ([prev] if prev else []))
                if k == 0:
                    prev = ctx.v(lambda e, S: e.tensor_copy(
                        out=mt, in_=df[0]), w=[t1])
                    prev = ctx.v(lambda e, S: e.tensor_copy(
                        out=mtn, in_=df[0]), w=[t1, prev])
                else:
                    prev = ctx.v(lambda e, S, d=d: e.tensor_tensor(
                        out=mt, in0=mt, in1=d, op=OP.max), w=[t1, prev])
                    prev = ctx.v(lambda e, S, d=d: e.tensor_tensor(
                        out=mtn, in0=mtn, in1=d, op=OP.min), w=[t1, prev])
            prev = ctx.v(lambda e, S: e.tensor_scalar(
                out=mtn, in0=mtn, scalar1=-1.0, scalar2=None, op0=OP.mult),
                w=[prev])
            prev = ctx.v(lambda e, S: e.tensor_tensor(
                out=mt, in0=mt, in1=mtn, op=OP.max), w=[prev])
            pra = amax_to_row(lambda tc8: mt[:, tc8 * 128:tc8 * 128 + 128],
                              [prev])
            tk = ctx.v(lambda e, S: e.tensor_tensor(
                out=row(r_am), in0=row(r_am), in1=row(r_sd), op=OP.mult),
                w=[pra, tok_rows])
            tk = ctx.v(lambda e, S: e.tensor_scalar(
                out=row(r_am), in0=row(r_am), scalar1=EPS_Q, scalar2=None,
                op0=OP.max), w=[tk])
            tk2 = ctx.v(lambda e, S: e.reciprocal(out=row(r_rs),
                                                  in_=row(r_am)), w=[tk])
            tk2 = ctx.v(lambda e, S: e.tensor_scalar(
                out=row(r_rs), in0=row(r_rs), scalar1=128.0, scalar2=None,
                op0=OP.mult), w=[tk2])
            tk2 = ctx.v(lambda e, S: e.tensor_tensor(
                out=row(r_rs), in0=row(r_rs), in1=row(r_sd), op=OP.mult),
                w=[tk2])
            tk3 = ctx.v(lambda e, S: e.tensor_tensor(
                out=row(r_mrs), in0=row(r_mu), in1=row(r_rs), op=OP.mult),
                w=[tk2])
            tk4 = ctx.v(lambda e, S: e.tensor_scalar(
                out=row(r_dq), in0=row(r_am), scalar1=1.0 / 128.0,
                scalar2=None, op0=OP.mult), w=[tk])
            tb0 = bcast(0, r_rs, [tk2, ctx.tok("v")])
            tb1 = bcast(1, r_mrs, [tk3])
            tb2 = bcast(2, r_dq, [tk4, ctx.tok("v")])
            parks = []
            qts = []
            for k in range(NF):
                d = df[k % 2]
                t1 = ctx.v(lambda e, S, k=k, d=d: e.tensor_tensor(
                    out=d, in0=xtile(k), in1=PS[0][:, :],
                    op=OP.mult), w=[tb0, tb1, tb2] +
                    ([qts[k - 2]] if k >= 2 else []))
                parks.append(ctx.dma("prkw", lambda e, S, k=k: e.dma_start(
                    out=dst_park[:, k * T:(k + 1) * T],
                    in_=xtile(k)), w=[t1]))
                t1 = ctx.v(lambda e, S, d=d: e.tensor_tensor(
                    out=d, in0=d, in1=PS[1][:, :], op=OP.subtract), w=[t1])
                t1 = ctx.v(lambda e, S, d=d: e.tensor_scalar(
                    out=d, in0=d, scalar1=MAGIC, scalar2=MAGIC, op0=OP.add,
                    op1=OP.subtract), w=[t1])
                t1 = ctx.v(lambda e, S, d=d: e.tensor_scalar(
                    out=d, in0=d, scalar1=127.0, scalar2=-128.0, op0=OP.min,
                    op1=OP.max), w=[t1])
                t1 = ctx.v(lambda e, S, k=k, d=d: e.tensor_tensor(
                    out=xq_t[:, k * T:(k + 1) * T], in0=d, in1=PS[2][:, :],
                    op=OP.mult), w=[t1, parks[k // 2]])
                qts.append(t1)
            tok_xq = qts[-1]

            xcopy = {}
            for m in range(64):
                p = m % 2
                gps, ups = PS[2 * p], PS[2 * p + 1]
                pw = [tok_cache, tok_xq] + ([xcopy[m - 2]] if m >= 2 else [])
                lg = ctx.dma("wgl", lambda e, S, m=m, p=p: e.dma_start(
                    out=wgu[:, p * 2048:p * 2048 + 2048],
                    in_=wq_g[:, m * 2048:(m + 1) * 2048]), w=pw)
                lu = ctx.dma("wgl", lambda e, S, m=m, p=p: e.dma_start(
                    out=wgu[:, 4096 + p * 2048:4096 + p * 2048 + 2048],
                    in_=wq_u[:, m * 2048:(m + 1) * 2048]), w=pw)
                mm = None
                for h2 in range(2):
                    for k in range(NF):
                        fn = (lambda e, S, p=p, h2=h2, k=k, gps=gps: e.matmul(
                            gps[:, h2 * 512:h2 * 512 + 512],
                            wgu[:, p * 2048 + k * 128:
                                p * 2048 + k * 128 + 128],
                            xq_t[:, k * T + h2 * 512: k * T + h2 * 512 + 512],
                            start=(k == 0), stop=(k == NF - 1)))
                        wv = [lg, tok_xq] + ([xcopy[m - 2]] if m >= 2 else [])
                        mm = ctx.t(fn, wv) if (h2 == 1 and k == NF - 1) \
                            else ctx.tq(fn, wv)
                for h2 in range(2):
                    for k in range(NF):
                        fn = (lambda e, S, p=p, h2=h2, k=k, ups=ups: e.matmul(
                            ups[:, h2 * 512:h2 * 512 + 512],
                            wgu[:, 4096 + p * 2048 + k * 128:
                                4096 + p * 2048 + k * 128 + 128],
                            xq_t[:, k * T + h2 * 512: k * T + h2 * 512 + 512],
                            start=(k == 0), stop=(k == NF - 1)))
                        mm = ctx.t(fn, [lu]) if (h2 == 1 and k == NF - 1) \
                            else ctx.tq(fn, [lu])
                sg = ctx.s(lambda e, S, p=p, gps=gps: e.activation(
                    out=vgu[:, p * 1024:p * 1024 + 1024],
                    in_=gps[:, :], func=AF.Silu, bias=0.0,
                    scale=wsc[:, 3:4]), w=[mm, tok_wsc])
                su = ctx.s(lambda e, S, p=p, ups=ups: e.mul(
                    out=vgu[:, 2048 + p * 1024:2048 + p * 1024 + 1024],
                    in_=ups[:, :], mul=wsc[:, 4:5]), w=[mm])
                vv = ctx.v(lambda e, S, p=p: e.tensor_tensor(
                    out=vgu[:, 4096 + p * 1024:4096 + p * 1024 + 1024],
                    in0=vgu[:, p * 1024:p * 1024 + 1024],
                    in1=vgu[:, 2048 + p * 1024:2048 + p * 1024 + 1024],
                    op=OP.mult), w=[sg, su])
                sb = None
                for h2 in range(2):
                    sb = ctx.t(lambda e, S, p=p, h2=h2, gps=gps: e.matmul(
                        gps[:, h2 * 512:h2 * 512 + 512], h128[:, :],
                        vgu[:, 4096 + p * 1024 + h2 * 512:
                            4096 + p * 1024 + h2 * 512 + 512],
                        start=True, stop=True), w=[vv])
                xcopy[m] = ctx.v(lambda e, S, m=m, gps=gps: e.tensor_copy(
                    out=arb_b[:, m * T:(m + 1) * T], in_=gps[:, :]),
                    w=[sb] + (ctx.fence("prkw") if m >= 48 else []))
            tok_gu = xcopy[63]

            SPARE = -1
            slot = list(range(64))
            spare_slot = SPARE

            def tap(sl):
                if sl == SPARE:
                    return spare_t[:, 0:1024]
                return arb_b[:, sl * T:(sl + 1) * T]
            prev_op = tok_gu
            for st in range(6):
                dist = 1 << st
                for grp in range(0, 64, 2 * dist):
                    for j in range(dist):
                        lo, hi = grp + j, grp + j + dist
                        slo, shi, ssp = slot[lo], slot[hi], spare_slot
                        prev_op = ctx.v(lambda e, S, slo=slo, shi=shi,
                                        ssp=ssp: e.tensor_tensor(
                            out=tap(ssp), in0=tap(slo), in1=tap(shi),
                            op=OP.subtract), w=[prev_op])
                        prev_op = ctx.v(lambda e, S, slo=slo,
                                        shi=shi: e.tensor_tensor(
                            out=tap(slo), in0=tap(slo), in1=tap(shi),
                            op=OP.add), w=[prev_op])
                        slot[hi] = ssp
                        spare_slot = shi
            tok_fa = prev_op

            mb = mbb[:, 0:1024]
            mbn = vgu[:, 0:1024]
            prev = None
            for k in range(64):
                sl = slot[k]
                if k == 0:
                    prev = ctx.v(lambda e, S, sl=sl: e.tensor_copy(
                        out=mb, in_=tap(sl)), w=[tok_fa])
                    prev = ctx.v(lambda e, S, sl=sl: e.tensor_copy(
                        out=mbn, in_=tap(sl)), w=[tok_fa, prev])
                else:
                    prev = ctx.v(lambda e, S, sl=sl: e.tensor_tensor(
                        out=mb, in0=mb, in1=tap(sl), op=OP.max), w=[prev])
                    prev = ctx.v(lambda e, S, sl=sl: e.tensor_tensor(
                        out=mbn, in0=mbn, in1=tap(sl), op=OP.min), w=[prev])
            mf = mff[:, 0:1024]
            prev = ctx.v(lambda e, S: e.tensor_scalar(
                out=mbn, in0=mbn, scalar1=-1.0, scalar2=None, op0=OP.mult),
                w=[prev])
            prev = ctx.v(lambda e, S: e.tensor_tensor(
                out=mb, in0=mb, in1=mbn, op=OP.max), w=[prev])
            prev = ctx.v(lambda e, S: e.tensor_copy(out=mf, in_=mb), w=[prev])
            pra = amax_to_row(lambda tc8: mf[:, tc8 * 128:tc8 * 128 + 128],
                              [prev])
            tk = ctx.v(lambda e, S: e.tensor_scalar(
                out=row(r_am), in0=row(r_am), scalar1=EPS_Q, scalar2=None,
                op0=OP.max), w=[pra])
            tk2 = ctx.v(lambda e, S: e.reciprocal(out=row(r_rs),
                                                  in_=row(r_am)), w=[tk])
            tk2 = ctx.v(lambda e, S: e.tensor_scalar(
                out=row(r_rs), in0=row(r_rs), scalar1=128.0, scalar2=None,
                op0=OP.mult), w=[tk2])
            tk4 = ctx.v(lambda e, S: e.tensor_scalar(
                out=row(r_dq), in0=row(r_am), scalar1=(0.5 * ISQ_I / 128.0),
                scalar2=ccsb[0:1, 10:11], op0=OP.mult, op1=OP.mult),
                w=[tk, tok_wsc])
            tb0 = bcast(0, r_rs, [tk2, ctx.tok("v")])
            tb2 = bcast(2, r_dq, [tk4, ctx.tok("v")])
            tb2c = ctx.v(lambda e, S: e.tensor_copy(out=scomb[:, 0:1024],
                                                    in_=PS[2][:, :]), w=[tb2])
            q8 = []
            for k in range(64):
                sl = slot[k]
                tf = qtf[:, (k % 2) * 1024:(k % 2) * 1024 + 1024]
                t1 = ctx.v(lambda e, S, sl=sl, tf=tf: e.tensor_tensor(
                    out=tf, in0=tap(sl), in1=PS[0][:, :], op=OP.mult),
                    w=[tb0, tb2] + ([q8[k - 2]] if k >= 2 else []))
                t1 = ctx.v(lambda e, S, tf=tf: e.tensor_scalar(
                    out=tf, in0=tf, scalar1=MAGIC, scalar2=MAGIC, op0=OP.add,
                    op1=OP.subtract), w=[t1])
                t1 = ctx.v(lambda e, S, sl=sl, tf=tf: e.tensor_scalar(
                    out=tap(sl), in0=tf, scalar1=127.0, scalar2=-128.0,
                    op0=OP.min, op1=OP.max), w=[t1])
                q8.append(t1)
            tok_q8 = q8[-1]

            dn = []
            last_park = None
            for m in range(16):
                p = m % 2
                dps = PS[1] if p else PS[3]
                wdt_ = {}
                for hb in range(2):
                    pw = [tok_q8] + ([dn[m - 2]] if m >= 2 else [])
                    wdt_[hb] = ctx.dma("wdn", lambda e, S, m=m, hb=hb:
                                       e.dma_start(
                        out=wdh[:, (m % 2) * 8192 + hb * 4096:
                                (m % 2) * 8192 + hb * 4096 + 4096],
                        in_=wq_d[:, m * 8192 + hb * 4096:
                                 m * 8192 + hb * 4096 + 4096]), w=pw)
                mm = None
                for hb in range(2):
                    for kk in range(32):
                        k = hb * 32 + kk
                        sl = slot[k]
                        for h2 in range(2):
                            fn = (lambda e, S, m=m, hb=hb, kk=kk, k=k, sl=sl,
                                  h2=h2, dps=dps: e.matmul(
                                dps[:, h2 * 512:h2 * 512 + 512],
                                wdh[:, (m % 2) * 8192 + hb * 4096 + kk * 128:
                                    (m % 2) * 8192 + hb * 4096 + kk * 128 + 128],
                                tap(sl)[:, h2 * 512:h2 * 512 + 512],
                                start=(k == 0), stop=(k == 63)))
                            wv = [wdt_[hb], tok_q8] + \
                                ([dn[m - 2]] if m >= 2 else [])
                            mm = ctx.t(fn, wv) if (k == 63 and h2 == 1) \
                                else ctx.tq(fn, wv)
                xs = xst[:, p * 1024:p * 1024 + 1024]
                es = est[:, 0:1024]
                lx = ctx.dma("prkr", lambda e, S, m=m, xs=xs: e.dma_start(
                    out=xs, in_=dst_park[:, m * T:(m + 1) * T]), w=[mm])
                ev = ctx.v(lambda e, S, es=es, dps=dps: e.tensor_tensor(
                    out=es, in0=dps[:, :], in1=scomb[:, 0:1024], op=OP.mult),
                    w=[mm, tb2c] + ([last_park] if last_park else []))
                ev = ctx.v(lambda e, S, es=es, xs=xs: e.tensor_tensor(
                    out=es, in0=es, in1=xs, op=OP.add), w=[ev, lx])
                dn.append(ev)
                last_park = ctx.dma("prkw", lambda e, S, m=m, es=es:
                                    e.dma_start(
                    out=dst_park[:, m * T:(m + 1) * T], in_=es), w=[ev])
            return last_park

        def emit_mamba(w_entry):
            tok_rows, tstat = ln_stats(w_entry, mlncast)
            tkm = ctx.v(lambda e, S: e.tensor_tensor(
                out=row(r_mrs), in0=row(r_mu), in1=row(r_sd), op=OP.mult),
                w=[tok_rows])
            tbr = bcast(0, r_sd, [tok_rows, tstat])
            tbm = bcast(1, r_mrs, [tkm])
            df = [mdff[:, 0:1024], mdff[:, 1024:2048]]
            xls = []
            for k in range(NF):
                d = df[k % 2]
                t1 = ctx.v(lambda e, S, k=k, d=d: e.tensor_tensor(
                    out=d, in0=xtile(k), in1=PS[0][:, :],
                    op=OP.mult), w=[tbr, tbm] +
                    ([xls[k - 2]] if k >= 2 else []))
                t1 = ctx.v(lambda e, S, k=k, d=d: e.tensor_tensor(
                    out=arb_b[:, k * 2048:k * 2048 + 1024], in0=d,
                    in1=PS[1][:, :], op=OP.subtract), w=[t1])
                xls.append(t1)
            xln_v = arb_b.ap().rearrange("c (k t) -> c k t", k=32)

            def e1w(e, S):
                for arm in e.switch_core_id(n=ARMN):
                    hh = int(arm) % 2
                    e.dma_start(out=s_xln[hh, :, :],
                                in_=xln_v[:, 0:16, 0:1024]).then_inc(
                        S["e1w0"], 16)
                return None
            t_e1w = ctx.emit("sync", e1w, w=xls, inc="e1w0", k=16)
            b1 = barrier(w=[t_e1w], pairs=True)
            ctx.dma("e1r", lambda e, S: e.dma_start(
                out=xln_v[:, 0:16, 0:1024], in_=s_xln[0, :, :]), w=[b1])
            tok_xln = ctx.dma("e1r", lambda e, S: e.dma_start(
                out=xln_v[:, 0:16, 1024:2048], in_=s_xln[1, :, :]), w=[b1])

            if msub < 2:
                return None
            win_tok = {}
            xz_mm = []
            ps_last = {0: None, 1: None}
            conv_last = [None]
            silu_last = [None]
            xcpk = {}
            zpk = {}
            for m in range(32):
                p = m % 2
                pw = [tok_xln] + ([win_tok[m - 2]] if m >= 2 else []) + \
                    ([xz_mm[-1]] if xz_mm else [])
                win_tok[m] = ctx.dma("winl", lambda e, S, m=m, p=p:
                                     e.dma_start(
                    out=winb[:, p * 2048:p * 2048 + 2048],
                    in_=d_win[:, m * 2048:(m + 1) * 2048]), w=pw)
                ch_toks = []
                for ch in range(4):
                    q = (m * 4 + ch) % 2
                    mm = None
                    for k in range(NF):
                        fn = (lambda e, S, p=p, ch=ch, k=k, q=q: e.matmul(
                            PS[q][:, 0:512],
                            winb[:, p * 2048 + k * 128:
                                 p * 2048 + k * 128 + 128],
                            arb_b[:, k * 2048 + ch * 512:
                                  k * 2048 + ch * 512 + 512],
                            start=(k == 0), stop=(k == NF - 1)))
                        wv = [win_tok[m], tok_xln] + \
                            ([ps_last[q]] if ps_last[q] else [])
                        mm = ctx.t(fn, wv) if k == NF - 1 else ctx.tq(fn, wv)
                    xz_mm.append(mm)
                    if m < 16:
                        ev = ctx.v(lambda e, S, ch=ch, q=q: e.tensor_copy(
                            out=xpscr[:, ch * 512:ch * 512 + 512],
                            in_=PS[q][:, 0:512]), w=[mm, conv_last[0]])
                    else:
                        ev = ctx.s(lambda e, S, ch=ch, q=q: e.activation(
                            out=zstg[:, ch * 512:ch * 512 + 512],
                            in_=PS[q][:, 0:512], func=AF.Silu, bias=0.0,
                            scale=1.0),
                            w=[mm] + ([zpk[m - 1]] if (m >= 17 and ch == 0)
                                      else []))
                    ch_toks.append(ev)
                    ps_last[q] = ev
                if m < 16:
                    c1 = ctx.v(lambda e, S, m=m: e.tensor_scalar(
                        out=cvscr[:, 0:2048], in0=xpscr[:, 0:2048],
                        scalar1=cwt[:, 3 * m + 1:3 * m + 2], scalar2=None,
                        op0=OP.mult), w=ch_toks + [silu_last[0]])
                    c1 = ctx.v(lambda e, S, m=m: e.scalar_tensor_tensor(
                        out=cvscr[:, 1:2048], in0=xpscr[:, 0:2047],
                        scalar=cwt[:, 3 * m:3 * m + 1],
                        in1=cvscr[:, 1:2048], op0=OP.mult, op1=OP.add),
                        w=[c1])
                    c1 = ctx.v(lambda e, S, m=m: e.scalar_tensor_tensor(
                        out=cvscr[:, 0:2047], in0=xpscr[:, 1:2048],
                        scalar=cwt[:, 3 * m + 2:3 * m + 3],
                        in1=cvscr[:, 0:2047], op0=OP.mult, op1=OP.add),
                        w=[c1])
                    conv_last[0] = c1
                    sl_ = silu_last[0] = ctx.s(lambda e, S, m=m, p=p: e.activation(
                        out=xcstg[:, p * 2048:p * 2048 + 2048],
                        in_=cvscr[:, 0:2048], func=AF.Silu,
                        bias=cbt[:, m:m + 1], scale=1.0),
                        w=[c1] + ([xcpk[m - 2]] if m >= 2 else []))
                    xcpk[m] = ctx.dma("xcp", lambda e, S, m=m, p=p:
                                      e.dma_start(
                        out=p_xc[:, m * 2048:(m + 1) * 2048],
                        in_=xcstg[:, p * 2048:p * 2048 + 2048]), w=[sl_])
                else:
                    zpk[m] = ctx.dma("zp", lambda e, S, m=m: e.dma_start(
                        out=p_z[:, (m - 16) * 2048:(m - 15) * 2048],
                        in_=zstg[:, 0:2048]), w=ch_toks)
            tok_xc = ctx.fence("xcp")

            if msub < 3:
                return None
            lwx = ctx.dma("wxl", lambda e, S: e.dma_start(
                out=wxs[:, 0:2560], in_=d_wx[:, :]),
                w=[win_tok[31], ctx.tok("s")])
            xcbt = {}
            mmd = None
            for k in range(NF):
                p = k % 2
                lk = ctx.dma("xcbr", lambda e, S, k=k, p=p: e.dma_start(
                    out=xcb[:, p * 2048:p * 2048 + 2048],
                    in_=p_xc[:, k * 2048:(k + 1) * 2048]),
                    w=tok_xc + [lwx] + ([xcbt[k - 2]] if k >= 2 else []))
                for nch in range(4):
                    fn = (lambda e, S, k=k, p=p, nch=nch: e.matmul(
                        PS[nch // 2][:, (nch % 2) * 512:(nch % 2) * 512 + 512],
                        wxs[:, k * 160: k * 160 + 128],
                        xcb[:, p * 2048 + nch * 512:
                            p * 2048 + nch * 512 + 512],
                        start=(k == 0), stop=(k == NF - 1)))
                    mmd = ctx.t(fn, [lk, lwx]) if (k == NF - 1 and nch == 3) \
                        else ctx.tq(fn, [lk, lwx])
                    fn2 = (lambda e, S, k=k, p=p, nch=nch: e.matmul(
                        PS[2 + nch // 2][0:32,
                                         (nch % 2) * 512:(nch % 2) * 512 + 512],
                        wxs[:, k * 160 + 128: k * 160 + 160],
                        xcb[:, p * 2048 + nch * 512:
                            p * 2048 + nch * 512 + 512],
                        start=(k == 0), stop=(k == NF - 1)))
                    mmd = ctx.t(fn2, [lk, lwx]) if nch == 3 \
                        else ctx.tq(fn2, [lk, lwx])
                xcbt[k] = mmd
            eva = ctx.v(lambda e, S: e.tensor_copy(
                out=xdbla[:, 0:1024], in_=PS[0][:, :]), w=[mmd])
            eva = ctx.v(lambda e, S: e.tensor_copy(
                out=xdbla[:, 1024:2048], in_=PS[1][:, :]), w=[mmd, eva])
            evb = ctx.v(lambda e, S: e.tensor_copy(
                out=t32f[:, 0:1024], in_=PS[2][0:32, :]), w=[mmd, eva])
            evb = ctx.v(lambda e, S: e.tensor_copy(
                out=t32f[:, 1024:2048], in_=PS[3][0:32, :]), w=[mmd, evb])

            def e2w(e, S):
                for arm in e.switch_core_id(n=ARMN):
                    hh = int(arm) % 2
                    e.dma_start(out=s_xdbl[hh, 0:128, :],
                                in_=xdbla[:, 0:2048]).then_inc(S["e2w0"], 16)
                    e.wait_ge(S["e2w0"], W_E2W[0] + 16)
                    e.dma_start(out=s_xdbl[hh, 128:160, :],
                                in_=t32f[:, 0:2048]).then_inc(S["e2w0"], 16)
                return None
            W_E2W = [ctx.cnt["e2w0"]]
            t_e2w = ctx.emit("sync", e2w, w=[eva, evb], inc="e2w0", k=32)
            b2 = barrier(w=[t_e2w], pairs=True)

            def e2r(e, S):
                for arm in e.switch_core_id(n=ARMN):
                    hh = int(arm) % 2
                    e.dma_start(out=xdbla[:, 0:2048],
                                in_=s_xdbl[1 - hh, 0:128, :],
                                accum_op=OP.add).then_inc(S["e2r0"], 16)
                    e.wait_ge(S["e2r0"], W_E2R[0] + 16)
                    e.dma_start(out=t32f[:, 0:2048],
                                in_=s_xdbl[1 - hh, 128:160, :],
                                accum_op=OP.add).then_inc(S["e2r0"], 16)
                return None
            W_E2R = [ctx.cnt["e2r0"]]
            t_e2r = ctx.emit("gpsimd", e2r, w=[b2], inc="e2r0", k=32)
            cdt = ctx.v(lambda e, S: e.tensor_copy(
                out=dtinb[:, 0:2048], in_=xdbla[:, 0:2048]), w=[t_e2r])
            cbc = ctx.v(lambda e, S: e.tensor_copy(
                out=bc_bf[:, 0:2048], in_=t32f[:, 0:2048]), w=[t_e2r])

            if msub < 4:
                return None
            lwdt = ctx.dma("wdtl", lambda e, S: e.dma_start(
                out=wdtb_s[:, 0:2048], in_=d_wdt[:, :]),
                w=[eva, ctx.tok("v"), ctx.tok("s")])
            dt_parks = []
            sp_prev = {}
            sp_scr = [None]
            import os as _os
            for m in range(NDT):
                p = m % 2
                sp_toks = []
                for nch in range(4):
                    q = (m * 4 + nch) % 2
                    import os as _os3
                    _mcalt = _os3.environ.get("MC_ALT", "0") == "1"
                    mm = ctx.t(lambda e, S, m=m, nch=nch, q=q: e.matmul(
                        PS[q][:, 0:512],
                        (winb if _mcalt else wdtb_s)[:, m * 128:m * 128 + 128],
                        dtinb[:, nch * 512:nch * 512 + 512],
                        start=True, stop=True),
                        w=[lwdt, cdt] +
                        ([sp_prev[(m * 4 + nch - 2)]]
                         if (m * 4 + nch) >= 2 else []))
                    v0 = ctx.v(lambda e, S, m=m, q=q: e.tensor_scalar(
                        out=das[:, 0:512], in0=PS[q][:, 0:512],
                        scalar1=bdtt[:, m:m + 1], scalar2=None, op0=OP.add),
                        w=[mm, sp_scr[0]])
                    s1 = ctx.s(lambda e, S: e.activation(
                        out=das[:, 512:1024], in_=das[:, 0:512], func=AF.Abs,
                        bias=0.0, scale=1.0), w=[v0])
                    s2 = ctx.s(lambda e, S: e.activation(
                        out=das[:, 1024:1536], in_=das[:, 512:1024],
                        func=AF.Exp, bias=0.0, scale=-1.0), w=[s1])
                    s3 = ctx.s(lambda e, S: e.activation(
                        out=das[:, 512:1024], in_=das[:, 1024:1536],
                        func=AF.Ln, bias=1.0, scale=1.0), w=[s2])
                    v1 = ctx.v(lambda e, S: e.tensor_scalar(
                        out=das[:, 1536:2048], in0=das[:, 0:512],
                        scalar1=0.0, scalar2=None, op0=OP.max), w=[v0, s1])
                    sp = ctx.v(lambda e, S, m=m, nch=nch, p=p:
                               e.tensor_tensor(
                        out=dtstg[:, p * 2048 + nch * 512:
                                  p * 2048 + nch * 512 + 512],
                        in0=das[:, 512:1024], in1=das[:, 1536:2048],
                        op=OP.add),
                        w=[s3, v1] + ([dt_parks[m - 2]] if (m >= 2 and
                                                            nch == 0) else []))
                    sp_scr[0] = sp
                    sp_prev[(m * 4 + nch)] = sp
                    sp_toks.append(sp)
                import os as _os2
                if _os2.environ.get("MC_NOPARK", "0") != "1":
                    dt_parks.append(ctx.dma("dtp", lambda e, S, m=m, p=p:
                                            e.dma_start(
                        out=p_dt[:, m * 2048:(m + 1) * 2048],
                        in_=dtstg[:, p * 2048:p * 2048 + 2048]),
                        w=sp_toks))
                else:
                    dt_parks.append(sp_toks[-1])
            tok_dt = ctx.fence("dtp")

            if msub < 5:
                return None
            wo_evac = []
            po_parks = []
            for th in range(2):
                fv = ctx.tok("v")
                ft = ctx.tok("t")
                fs = ctx.tok("s")
                sdt = []
                sxc = []
                for m in range(NDT):
                    sdt.append(ctx.dma("sdt", lambda e, S, m=m, th=th:
                                       e.dma_start(
                        out=arb_b[:, m * T:(m + 1) * T],
                        in_=p_dt[:, m * 2048 + th * T:
                                 m * 2048 + th * T + T]),
                        w=[tok_dt, fv, ft, fs]))
                    sxc.append(ctx.dma("sxc", lambda e, S, m=m, th=th:
                                       e.dma_start(
                        out=arb_b[:, 16384 + m * T:16384 + (m + 1) * T],
                        in_=p_xc[:, m * 2048 + th * T:
                                 m * 2048 + th * T + T]),
                        w=tok_xc + [fv, ft, fs]))
                du_t = []
                for m in range(NDT):
                    du_t.append(ctx.v(lambda e, S, m=m: e.tensor_tensor(
                        out=arb_b[:, 32768 + m * T:32768 + (m + 1) * T],
                        in0=arb_b[:, m * T:(m + 1) * T],
                        in1=arb_b[:, 16384 + m * T:16384 + (m + 1) * T],
                        op=OP.mult), w=[sdt[m], sxc[m], fv]))
                ytoks = {}
                bc_prev = None
                for sp_i in range(8):
                    s0, s1 = 2 * sp_i, 2 * sp_i + 1
                    btoks = []
                    for js, sel in enumerate((s0, 16 + s0, s1, 16 + s1)):
                        mmb = None
                        for h2 in range(2):
                            fn = (lambda e, S, sel=sel, h2=h2, th=th:
                                  e.matmul(
                                PS[2][:, h2 * 512:h2 * 512 + 512],
                                eall[:, sel * 128:(sel + 1) * 128],
                                bc_bf[:, th * T + h2 * 512:
                                      th * T + h2 * 512 + 512],
                                start=True, stop=True))
                            wv = [cbc, ft] + ([bc_prev] if bc_prev else [])
                            mmb = ctx.t(fn, wv) if h2 == 1 else ctx.tq(fn, wv)
                        bc_prev = ctx.v(lambda e, S, js=js: e.tensor_copy(
                            out=bbcb[:, js * 1024:js * 1024 + 1024],
                            in_=PS[2][:, :]), w=[mmb])
                        btoks.append(bc_prev)
                    for m in range(NDT):
                        pps = []
                        sc_prev = None
                        for jj, s_ in enumerate((s0, s1)):
                            da = ctx.s(lambda e, S, m=m, s_=s_, jj=jj:
                                       e.activation(
                                out=das[:, jj * 1024:jj * 1024 + 1024],
                                in_=arb_b[:, m * T:(m + 1) * T], func=AF.Exp,
                                bias=0.0,
                                scale=acol[:, m * SS + s_:m * SS + s_ + 1]),
                                w=[sdt[m], ctx.tok("v")])
                            wk = ctx.v(lambda e, S, m=m, jj=jj:
                                       e.tensor_tensor(
                                out=wsc_[:, 0:1024],
                                in0=arb_b[:, 32768 + m * T:
                                          32768 + (m + 1) * T],
                                in1=bbcb[:, 2 * jj * 1024:
                                         2 * jj * 1024 + 1024],
                                op=OP.mult),
                                w=[du_t[m], btoks[2 * jj]] +
                                ([sc_prev] if sc_prev else []))
                            idx = m * SS + s_
                            sc_ = ctx.v(lambda e, S, jj=jj, idx=idx:
                                        e.tensor_tensor_scan(
                                out=hsc[:, 0:1024],
                                data0=das[:, jj * 1024:jj * 1024 + 1024],
                                data1=wsc_[:, 0:1024],
                                initial=states[:, idx:idx + 1],
                                op0=OP.mult, op1=OP.add),
                                w=[da, wk] + (pps[-1:] if pps else []))
                            sv = ctx.v(lambda e, S, idx=idx: e.tensor_copy(
                                out=states[:, idx:idx + 1],
                                in_=hsc[:, 1023:1024]), w=[sc_])
                            pp = ctx.v(lambda e, S, jj=jj: e.tensor_tensor(
                                out=(p0s if jj == 0 else p1s)[:, 0:1024],
                                in0=hsc[:, 0:1024],
                                in1=bbcb[:, (2 * jj + 1) * 1024:
                                         (2 * jj + 1) * 1024 + 1024],
                                op=OP.mult),
                                w=[sc_, sv, btoks[2 * jj + 1]])
                            pps.append(pp)
                            sc_prev = sc_
                        pr = ctx.v(lambda e, S: e.tensor_tensor(
                            out=p0s[:, 0:1024], in0=p0s[:, 0:1024],
                            in1=p1s[:, 0:1024], op=OP.add), w=pps)
                        if sp_i == 0:
                            ytoks[m] = ctx.v(lambda e, S, m=m: e.tensor_copy(
                                out=arb_b[:, 49152 + m * T:
                                          49152 + (m + 1) * T],
                                in_=p0s[:, 0:1024]), w=[pr, fv, ft])
                        else:
                            ytoks[m] = ctx.v(lambda e, S, m=m:
                                             e.tensor_tensor(
                                out=arb_b[:, 49152 + m * T:
                                          49152 + (m + 1) * T],
                                in0=arb_b[:, 49152 + m * T:
                                          49152 + (m + 1) * T],
                                in1=p0s[:, 0:1024], op=OP.add),
                                w=[pr, ytoks[m]])
                gat = {}
                for m in range(NDT):
                    p = m % 2
                    zl = ctx.dma("zl", lambda e, S, m=m, th=th, p=p:
                                 e.dma_start(
                        out=zst[:, p * 1024:p * 1024 + 1024],
                        in_=p_z[:, m * 2048 + th * T: m * 2048 + th * T + T]),
                        w=ctx.fence("zp") + [ytoks[m]] +
                        ([gat[m - 2]] if m >= 2 else []))
                    g1 = ctx.v(lambda e, S, m=m: e.scalar_tensor_tensor(
                        out=arb_b[:, 49152 + m * T:49152 + (m + 1) * T],
                        in0=arb_b[:, 16384 + m * T:16384 + (m + 1) * T],
                        scalar=ddt[:, m:m + 1],
                        in1=arb_b[:, 49152 + m * T:49152 + (m + 1) * T],
                        op0=OP.mult, op1=OP.add), w=[ytoks[m], ytoks[NDT - 1]])
                    gat[m] = ctx.v(lambda e, S, m=m, p=p: e.tensor_tensor(
                        out=arb_b[:, 49152 + m * T:49152 + (m + 1) * T],
                        in0=arb_b[:, 49152 + m * T:49152 + (m + 1) * T],
                        in1=zst[:, p * 1024:p * 1024 + 1024],
                        op=OP.mult), w=[g1, zl])
                wo_tok = {}
                for m in range(NF):
                    p = m % 2
                    pw = [gat[NDT - 1], gat[NDT - 2]] + \
                        ([wo_tok[m - 2]] if m >= 2 else []) + \
                        (wo_evac[-2:-1] if len(wo_evac) >= 2 else [])
                    wo_tok[m] = ctx.dma("wol", lambda e, S, m=m, p=p:
                                        e.dma_start(
                        out=woutb_s[:, p * 2048:p * 2048 + 2048],
                        in_=d_wout[:, m * 2048:(m + 1) * 2048]), w=pw)
                    dps = PS[p]
                    mm = None
                    for k in range(NDT):
                        for h2 in range(2):
                            fn = (lambda e, S, m=m, p=p, k=k, h2=h2, dps=dps:
                                  e.matmul(
                                dps[:, h2 * 512:h2 * 512 + 512],
                                woutb_s[:, p * 2048 + k * 128:
                                        p * 2048 + k * 128 + 128],
                                arb_b[:, 49152 + k * T + h2 * 512:
                                       49152 + k * T + h2 * 512 + 512],
                                start=(k == 0), stop=(k == NDT - 1)))
                            wv = [wo_tok[m], gat[NDT - 1]] + \
                                (wo_evac[-2:-1] if len(wo_evac) >= 2 else [])
                            mm = ctx.t(fn, wv) if (k == NDT - 1 and h2 == 1) \
                                else ctx.tq(fn, wv)
                    ev = ctx.v(lambda e, S, m=m, p=p, dps=dps: e.tensor_copy(
                        out=postg[:, p * 1024:p * 1024 + 1024],
                        in_=dps[:, :]),
                        w=[mm] + (po_parks[-2:-1] if len(po_parks) >= 2
                                  else []))
                    wo_evac.append(ev)
                    po_parks.append(ctx.dma(
                        "pop", lambda e, S, m=m, p=p, th=th: e.dma_start(
                            out=p_po[:, m * 2048 + th * T:
                                     m * 2048 + th * T + T],
                            in_=postg[:, p * 1024:
                                      p * 1024 + 1024]), w=[ev]))
            tok_po = ctx.fence("pop")

            if msub < 6:
                return None
            def e3w(e, S):
                for arm in e.switch_core_id(n=ARMN):
                    hh = int(arm) % 2
                    e.dma_start(out=s_po[hh, :, :],
                                in_=po_v[:, :, (1 - hh) * T:(2 - hh) * T]
                                ).then_inc(S["e3w0"], 16)
                return None
            t_e3w = ctx.emit("sync", e3w, w=tok_po, inc="e3w0", k=16)
            b3 = barrier(w=[t_e3w], pairs=True)
            if _os.environ.get("E3_NOR", "0") == "1":
                return None
            xlo_v = x_lo[:, :].rearrange("c (m t) -> c m t", m=8)
            xhi_v = x_hi[:, :].rearrange("c (m t) -> c m t", m=8)

            _e3rn = int(_os.environ.get("E3R_N", "6"))

            def e3r(e, S):
                for arm in e.switch_core_id(n=ARMN):
                    hh = int(arm) % 2
                    base = W_E3R[0]
                    todo = _e3rn
                    dmas = [
                        lambda: e.dma_start(out=x_lo[:, :],
                                            in_=s_po[1 - hh, :, 0:8 * T]),
                        lambda: e.dma_start(out=x_hi[:, :],
                                            in_=s_po[1 - hh, :, 8 * T:16 * T]),
                        lambda: e.dma_start(out=xlo_v[:, :, :],
                                            in_=po_v[:, 0:8,
                                                     hh * T:(hh + 1) * T],
                                            accum_op=OP.add),
                        lambda: e.dma_start(out=xhi_v[:, :, :],
                                            in_=po_v[:, 8:16,
                                                     hh * T:(hh + 1) * T],
                                            accum_op=OP.add),
                        lambda: e.dma_start(out=x_lo[:, :], in_=p_x[:, 0:8 * T],
                                            accum_op=OP.add,
                                            max_dma_last_dim=2048),
                        lambda: e.dma_start(out=x_hi[:, :],
                                            in_=p_x[:, 8 * T:16 * T],
                                            accum_op=OP.add,
                                            max_dma_last_dim=2048),
                    ]
                    for j in range(6):
                        if j < _e3rn:
                            dmas[j]().then_inc(S["e3r0"], 16)
                        else:
                            e.dma_start(out=x_lo[:, 0:64],
                                        in_=p_x[:, 0:64]).then_inc(
                                S["e3r0"], 16)
                        if j % 2 == 1:
                            e.wait_ge(S["e3r0"], base + 16 * (j + 1))
                return None
            W_E3R = [ctx.cnt["e3r0"]]
            tok_x2 = ctx.emit("gpsimd", e3r,
                              w=[b3, ctx.tok("v"), ctx.tok("s"),
                                 ctx.tok("t")], inc="e3r0", k=96)
            return tok_x2

        final_src = p_x
        tok_x1 = emit_ffn(p_x, tok_x_in)
        if stage >= 2:
            lds = []
            for k in range(NF):
                lds.append(ctx.dma("prkr", lambda e, S, k=k: e.dma_start(
                    out=xtile(k),
                    in_=p_x[:, k * T:(k + 1) * T]),
                    w=[tok_x1, ctx.tok("v"), ctx.tok("t"), ctx.tok("s")]))
            tok_x2 = emit_mamba(ctx.fence("prkr"))
            if tok_x2 is None:
                stage = 1
                final_src = p_x
            if stage >= 3:
                tok_x3 = emit_ffn(p_x3, tok_x2)
                final_src = p_x3
            else:
                for k in range(NF):
                    ctx.dma("prkw", lambda e, S, k=k: e.dma_start(
                        out=p_x3[:, k * T:(k + 1) * T],
                        in_=xtile(k)), w=[tok_x2])
                final_src = p_x3
        else:
            final_src = p_x

        otp = []
        base_w = (ctx.fence("prkw") + ctx.fence("e3r") +
                  [ctx.tok("v"), ctx.tok("t"), ctx.tok("s")])
        for tc in range(8):
            for fc in range(NF):
                k = tc * NF + fc
                ld = ctx.dma("outl", lambda e, S, tc=tc, fc=fc, k=k:
                             e.dma_start(
                    out=arb_f[:, (k % 4) * 128:(k % 4) * 128 + 128],
                    in_=final_src[:, fc * T + tc * 128:
                                  fc * T + tc * 128 + 128]),
                    w=base_w + ([otp[k - 4]] if k >= 4 else []))
                tt = ctx.t(lambda e, S, k=k: e.transpose(
                    out=PS[k % 2][:, 0:128],
                    in_=arb_f[:, (k % 4) * 128:(k % 4) * 128 + 128],
                    identity=id128[:, :]),
                    w=[ld] + ([otp[k - 2]] if k >= 2 else []))
                otp.append(ctx.v(lambda e, S, tc=tc, fc=fc, k=k: e.tensor_copy(
                    out=arb_f[:, 2048 + tc * 2048 + fc * 128:
                           2048 + tc * 2048 + fc * 128 + 128],
                    in_=PS[k % 2][:, 0:128]), w=[tt]))
        for tc in range(8):
            ctx.dma("outs", lambda e, S, tc=tc: e.dma_start(
                out=d_out[tc * 128:(tc + 1) * 128, :],
                in_=arb_f[:, 2048 + tc * 2048: 2048 + (tc + 1) * 2048]),
                w=[otp[tc * NF + NF - 1]])
        ctx.emit("sync", lambda e, S: None, w=ctx.fence("outs"))

        for _sn in ctx.semnames:
            sems[_sn] = _stk.enter_context(nc.semaphore(_sn))
        block = _stk.enter_context(nc.Block())

        def run(e, eng):
            for fn, wl, inc in ctx.q[eng]:
                for sname, vv in wl:
                    e.wait_ge(sems[sname], vv)
                inst = fn(e, sems)
                if inc and inst is not None:
                    inst.then_inc(sems[inc[0]], inc[1])

        @block.sync
        def _(e):
            run(e, "sync")

        @block.tensor
        def _(e):
            run(e, "tensor")

        @block.vector
        def _(e):
            run(e, "vector")

        @block.scalar
        def _(e):
            run(e, "scalar")

        @block.gpsimd
        def _(e):
            run(e, "gpsimd")

    return nc



def _lhsT_flat(w):
    O, I_ = w.shape
    nm, nk = O // 128, I_ // 128
    return np.ascontiguousarray(
        w.reshape(nm, 128, nk, 128).transpose(3, 0, 2, 1).reshape(128, -1))


def _lhsT_flat_in(w):
    I_, O = w.shape
    nk, nm = I_ // 128, O // 128
    return np.ascontiguousarray(
        w.reshape(nk, 128, nm, 128).transpose(1, 2, 0, 3).reshape(128, -1))


def make_in_maps(inputs):
    f32 = np.float32
    x = np.asarray(inputs["L0_attn_x"], f32)
    wg = np.asarray(inputs["L0_ffn_w_gate"], f32)
    wu = np.asarray(inputs["L0_ffn_w_up"], f32)
    wd = np.asarray(inputs["L0_ffn_w_down"], f32)
    win = np.asarray(inputs["L0_attn_W_in"], f32)
    wx = np.asarray(inputs["L0_attn_W_x"], f32)
    wdt = np.asarray(inputs["L0_attn_W_dt"], f32)
    wout = np.asarray(inputs["L0_attn_W_out"], f32)
    cw = np.asarray(inputs["L0_attn_conv_w"], f32)
    cb = np.asarray(inputs["L0_attn_conv_b"], f32)
    al = np.asarray(inputs["L0_attn_A_log"], f32)
    dd = np.asarray(inputs["L0_attn_D"], f32)
    bdt = np.asarray(inputs["L0_attn_b_dt"], f32)

    fg, fu, fd = _lhsT_flat(wg), _lhsT_flat(wu), _lhsT_flat(wd)

    h = np.array([[1.0]], f32)
    while h.shape[0] < 128:
        h = np.block([[h, h], [h, -h]])
    ident = np.eye(128, dtype=f32)
    eallm = np.zeros((32, 32 * 128), bfdt)
    for j in range(32):
        eallm[j, j * 128:(j + 1) * 128] = 1.0

    def tile_col(v):
        return np.ascontiguousarray(v.reshape(NDT, 128).T)

    win_b, wout_b, wx_b = {}, {}, {}
    for g in range(2):
        cols = np.concatenate([np.arange(g * DH, (g + 1) * DH),
                               np.arange(D + g * DH, D + (g + 1) * DH)])
        win_b[g] = _lhsT_flat_in(np.ascontiguousarray(win[:, cols]))
        wout_b[g] = _lhsT_flat_in(
            np.ascontiguousarray(wout[g * DH:(g + 1) * DH, :]))
        wx_b[g] = np.ascontiguousarray(
            wx[g * DH:(g + 1) * DH, :].reshape(NDT, 128, 160)
            .transpose(1, 0, 2).reshape(128, -1))

    maps = []
    for c in range(NC):
        beta, rho = c // 2, c % 2
        g = rho
        sl = np.s_[c * 16384:(c + 1) * 16384]
        p1 = np.ascontiguousarray(
            np.concatenate([fg[:, sl], fu[:, sl], fd[:, sl]], axis=1))
        chs = slice(g * DH, (g + 1) * DH)
        cwv = cw[chs, 0, :]
        cwt_ = np.zeros((128, 3 * NDT), f32)
        for m in range(NDT):
            for j in range(3):
                cwt_[:, 3 * m + j] = cwv[m * 128:(m + 1) * 128, j]
        alv = al[chs, :]
        alt_ = np.zeros((128, SS * NDT), f32)
        for m in range(NDT):
            alt_[:, m * SS:(m + 1) * SS] = alv[m * 128:(m + 1) * 128, :]
        maps.append({
            "x0": np.ascontiguousarray(x[beta, rho * T:(rho + 1) * T, :]),
            "wgb": fg, "wub": fu, "wdb": fd,
            "winb": win_b[g], "woutb": wout_b[g], "wxb": wx_b[g],
            "wdtb": np.ascontiguousarray(wdt[:, chs]),
            "cwt": cwt_, "cbt": tile_col(cb[chs]), "ddt": tile_col(dd[chs]),
            "bdtt": tile_col(bdt[chs]), "alt": alt_,
            "ident": ident, "h128": h.astype(bfdt),
            "onesc": np.ones((128, 2), f32),
            "onesb": np.ones((128, 2), bfdt),
            "onerowf": np.ones((1, 128), f32),
            "eall": eallm, "p1chunk": p1,
        })
    return maps


def kernel(**inputs):
    nc = build_program(STAGE)
    in_maps = make_in_maps(inputs)
    res = run_bass_kernel_spmd(nc, in_maps, core_ids=list(range(NC)))
    out = np.empty((B, L, H), np.float32)
    for c in range(NC):
        beta, rho = c // 2, c % 2
        out[beta, rho * T:(rho + 1) * T, :] = np.asarray(
            res.results[c]["xout"])
    return out


# version 07
# speedup vs baseline: 1.0218x; 1.0218x over previous
import numpy as np
import ml_dtypes
import concourse.bass as bass
import concourse.mybir as mybir
from concourse.bass_utils import run_bass_kernel_spmd

F32 = mybir.dt.float32
BF16 = mybir.dt.bfloat16
AF = mybir.ActivationFunctionType
OP = mybir.AluOpType
AX = mybir.AxisListType
bfdt = ml_dtypes.bfloat16

NC = 8
B, L, H = 4, 2048, 2048
D, DH = 4096, 2048
SS, RR = 16, 128
II = 8192
T = 1024
NF = 16
NDT = 16
MAGIC = float(np.float32(1.5 * 2.0 ** 23))
INV_H = 1.0 / float(H)
ISQ_I = 1.0 / float(np.sqrt(np.float32(II)))
EPS_LN = 1e-5
EPS_Q = 1e-5
STAGE = 3
MSUB = 9
ARMN = 128

ENGS = ("tensor", "vector", "scalar", "sync", "gpsimd")


class Ctx:

    def __init__(self):
        self.q = {e: [] for e in ENGS}
        self.cnt = {}
        self.wm = {e: {} for e in ENGS}
        self.streams = {}
        self.semnames = []
        for nm in ("t", "v", "s"):
            self.cnt[nm] = 0
            self.semnames.append(nm)

    def stream(self, name, eng, width=2):
        sl = []
        for j in range(width):
            sn = f"{name}{j}"
            self.cnt[sn] = 0
            self.semnames.append(sn)
            sl.append(sn)
        self.streams[name] = [eng, sl, 0]

    def csem(self, name):
        self.cnt[name] = 0
        self.semnames.append(name)

    def emit(self, eng, fn, w=(), inc=None, k=1):
        wl = []
        flat = []
        for tokn in w:
            if tokn is None:
                continue
            if isinstance(tokn, list):
                flat.extend(t for t in tokn if t is not None)
            else:
                flat.append(tokn)
        for tokn in flat:
            sname, vv = tokn
            if vv <= 0:
                continue
            if self.wm[eng].get(sname, 0) < vv:
                self.wm[eng][sname] = vv
                wl.append((sname, vv))
        tok = None
        if inc is not None:
            self.cnt[inc] += k
            tok = (inc, self.cnt[inc])
        self.q[eng].append((fn, wl, (inc, k) if inc else None))
        return tok

    def dma(self, stream, fn, w=()):
        eng, sl, idx = self.streams[stream]
        sn = sl[idx % len(sl)]
        self.streams[stream][2] += 1
        wl = list(w) + [(sn, self.cnt[sn])]
        return self.emit(eng, fn, wl, inc=sn, k=16)

    def fence(self, stream):
        sl = self.streams[stream][1]
        return [(sn, self.cnt[sn]) for sn in sl]

    def t(self, fn, w=()):
        return self.emit("tensor", fn, w, "t")

    def tq(self, fn, w=()):
        return self.emit("tensor", fn, w, None)

    def v(self, fn, w=()):
        return self.emit("vector", fn, w, "v")

    def s(self, fn, w=()):
        return self.emit("scalar", fn, w, "s")

    def tok(self, sem):
        return (sem, self.cnt[sem])


def build_program(stage=STAGE, msub=None):
    if msub is None:
        msub = MSUB
    nc = bass.Bass("TRN2", target_bir_lowering=False, debug=False)

    d_x0 = nc.dram_tensor("x0", [T, H], F32, kind="ExternalInput")
    d_out = nc.dram_tensor("xout", [T, H], F32, kind="ExternalOutput")
    d_wg = nc.dram_tensor("wgb", [128, 64 * 2048], F32, kind="ExternalInput")
    d_wu = nc.dram_tensor("wub", [128, 64 * 2048], F32, kind="ExternalInput")
    d_wd = nc.dram_tensor("wdb", [128, 16 * 8192], F32, kind="ExternalInput")
    d_win = nc.dram_tensor("winb", [128, 32 * 2048], F32, kind="ExternalInput")
    d_wout = nc.dram_tensor("woutb", [128, 16 * 2048], F32, kind="ExternalInput")
    d_wx = nc.dram_tensor("wxb", [128, 16 * 160], F32, kind="ExternalInput")
    d_wdt = nc.dram_tensor("wdtb", [128, DH], F32, kind="ExternalInput")
    d_cw = nc.dram_tensor("cwt", [128, 3 * NDT], F32, kind="ExternalInput")
    d_cb = nc.dram_tensor("cbt", [128, NDT], F32, kind="ExternalInput")
    d_dd = nc.dram_tensor("ddt", [128, NDT], F32, kind="ExternalInput")
    d_bdt = nc.dram_tensor("bdtt", [128, NDT], F32, kind="ExternalInput")
    d_al = nc.dram_tensor("alt", [128, SS * NDT], F32, kind="ExternalInput")
    d_id = nc.dram_tensor("ident", [128, 128], F32, kind="ExternalInput")
    d_h128 = nc.dram_tensor("h128", [128, 128], BF16, kind="ExternalInput")
    d_ones = nc.dram_tensor("onesc", [128, 2], F32, kind="ExternalInput")
    d_onesb = nc.dram_tensor("onesb", [128, 2], BF16, kind="ExternalInput")
    d_onerf = nc.dram_tensor("onerowf", [1, 128], F32, kind="ExternalInput")
    d_eall = nc.dram_tensor("eall", [32, 32 * 128], BF16, kind="ExternalInput")
    d_p1 = nc.dram_tensor("p1chunk", [128, 3 * 16384], F32, kind="ExternalInput")

    p_x = nc.dram_tensor("prk_x", [128, NF * T], F32)
    p_x3 = nc.dram_tensor("prk_x3", [128, NF * T], F32)
    p_z = nc.dram_tensor("prk_z", [128, NDT * L], BF16)
    p_xc = nc.dram_tensor("prk_xc", [128, NDT * L], BF16)
    p_dt = nc.dram_tensor("prk_dt", [128, NDT * L], BF16)
    p_po = nc.dram_tensor("prk_po", [128, NF * L], F32)
    wq_g = nc.dram_tensor("wq_g", [128, 64 * 2048], BF16)
    wq_u = nc.dram_tensor("wq_u", [128, 64 * 2048], BF16)
    wq_d = nc.dram_tensor("wq_d", [128, 16 * 8192], BF16)
    cc_mi = nc.dram_tensor("cc_mi", [1, 8], F32)
    cc_mo = nc.dram_tensor("cc_mo", [1, 8], F32)
    NBAR = 8
    cc_bi = [nc.dram_tensor(f"cc_bi{i}", [1, 8], F32) for i in range(NBAR)]
    cc_bo = [nc.dram_tensor(f"cc_bo{i}", [1, 8], F32) for i in range(NBAR)]

    s_xln = nc.dram_tensor("shr_xln", [2, 128, NF * T], BF16, addr_space="Shared")
    s_xdbl = nc.dram_tensor("shr_xdbl", [2, 160, L], F32, addr_space="Shared")
    s_po = nc.dram_tensor("shr_po", [2, 128, NF * T], F32, addr_space="Shared")

    po_v = p_po.ap().rearrange("c (m t) -> c m t", m=NF)

    off = [0]
    MBASE = 16512
    nc.alloc_sbuf_tensor("arena_rsv", [128, 53008], F32)

    def salloc(name, shape, dtype, at=None):
        a = off[0] if at is None else at
        h = nc.alloc_sbuf_tensor_at(name, shape, dtype, offset=MBASE + a)
        if at is None:
            sz = int(np.prod(shape[1:])) * mybir.dt.size(dtype)
            off[0] = a + ((sz + 31) // 32) * 32
        return h

    id128 = salloc("id128", [128, 128], F32)
    h128 = salloc("h128s", [128, 128], BF16)
    ones_c = salloc("ones_c", [128, 2], F32)
    ones_cb = salloc("ones_cb", [128, 2], BF16)
    ones_rf = salloc("ones_rf", [1, 128], F32)
    eall = salloc("ealls", [32, 32 * 128], BF16)
    cwt = salloc("cwts", [128, 3 * NDT], F32)
    cbt = salloc("cbts", [128, NDT], F32)
    ddt = salloc("ddts", [128, NDT], F32)
    bdtt = salloc("bdtts", [128, NDT], F32)
    alt = salloc("alts", [128, SS * NDT], F32)
    acol = salloc("acols", [128, SS * NDT], F32)
    wsc = salloc("wscs", [128, 8], F32)
    colb = salloc("colb", [128, 16], F32)
    states = salloc("states", [128, NDT * SS], F32)
    ccsb = salloc("ccsb", [1, 16], F32)
    assert off[0] <= 13280, off[0]
    spare_t = salloc("spare_t", [128, 1024], BF16, at=13280)

    AXL = 15360
    AXH = AXL + 32 * 1024
    AB0 = AXH + 32 * 1024
    assert MBASE + AB0 + 128 * 1024 <= 228500
    x_lo = salloc("x_lo", [128, 8 * T], F32, at=AXL)
    xq_t = salloc("xq_t", [128, 16 * T], BF16, at=AXL)
    qtf = salloc("qtf", [128, 2048], F32, at=AXL)
    mbb = salloc("mbb", [128, 1024], BF16, at=AXL + 8192)
    mff = salloc("mff", [128, 1024], F32, at=AXL + 10240)
    wdh = salloc("wdh", [128, 16384], BF16, at=AXL)
    winb = salloc("winb", [128, 4096], BF16, at=AXL)
    xpscr = salloc("xpscr", [128, 2048], F32, at=AXL + 8192)
    cvscr = salloc("cvscr", [128, 2048], F32, at=AXL + 16384)
    xcstg = salloc("xcstg", [128, 4096], BF16, at=AXL + 24576)
    xcb = salloc("xcb", [128, 4096], BF16, at=AXL)
    xdbla = salloc("xdbla", [128, 2048], F32, at=AXL + 8192)
    dtinb = salloc("dtinb", [128, 2048], BF16, at=AXL + 16384)
    t32f = salloc("t32f", [32, 2048], F32, at=AXL + 20480)
    wdtb_s = salloc("wdtb_s", [128, 2048], BF16, at=AXL + 28672)
    das = salloc("das", [128, 2048], F32, at=AXL)
    wsc_ = salloc("wsc_", [128, 1024], BF16, at=AXL + 8192)
    hsc = salloc("hsc", [128, 1024], BF16, at=AXL + 10240)
    p0s = salloc("p0s", [128, 1024], BF16, at=AXL + 12288)
    p1s = salloc("p1s", [128, 1024], BF16, at=AXL + 14336)
    bbcb = salloc("bbcb", [128, 4096], BF16, at=AXL + 16384)
    zst = salloc("zst", [128, 2048], BF16, at=AXL + 24576)
    rowb = salloc("rowb", [1, 8 * T], F32, at=AXH)
    wgu = salloc("wgu", [128, 8192], BF16, at=AXH)
    vgu = salloc("vgu", [128, 6144], BF16, at=AXH + 16384)
    xst = salloc("xst", [128, 2048], F32, at=AXH)
    est = salloc("est", [128, 1024], F32, at=AXH + 8192)
    scomb = salloc("scomb", [128, 1024], F32, at=AXH + 12288)
    zstg = salloc("zstg", [128, 2048], BF16, at=AXH)
    wxs = salloc("wxs", [128, 2560], BF16, at=AXH + 4096)
    dtstg = salloc("dtstg", [128, 4096], BF16, at=AXH + 9216)
    woutb_s = salloc("woutb_s", [128, 4096], BF16, at=AXH)
    postg = salloc("postg", [128, 2048], F32, at=AXH + 8192)
    bc_bf = salloc("bc_bf", [32, 2048], BF16, at=AXH + 17664)
    arb_b = salloc("arb_b", [128, 64 * T], BF16, at=AB0)
    arb_f = salloc("arb_f", [128, 32 * T], F32, at=AB0)
    p2st = salloc("p2st", [128, 4096], F32, at=AB0)
    p1st = salloc("p1st", [128, 4352], F32, at=AB0 + 16384)
    p2wq = salloc("p2wq", [128, 4096], BF16, at=AB0 + 81920)
    lncast = salloc("lncast", [128, 4096], BF16, at=AB0 + 41984)
    mtt2 = salloc("mtt2", [128, 1024], F32, at=AB0 + 41984)
    dff = salloc("dff", [128, 2048], F32, at=AB0 + 50176)
    mtt = salloc("mtt", [128, 1024], F32, at=AB0 + 58368)
    mlncast = salloc("mlncast", [128, 4096], BF16, at=AB0 + 65536)
    mdff = salloc("mdff", [128, 2048], F32, at=AB0 + 73728)
    x_hi = salloc("x_hi", [128, 8 * T], F32, at=AB0 + 98304)

    def xtile(k):
        if k < 8:
            return x_lo[:, k * T:(k + 1) * T]
        return x_hi[:, (k - 8) * T:(k - 7) * T]

    ctx = Ctx()
    for nm, eng, wd in (
        ("cst", "sync", 1), ("xin", "sync", 1), ("prkw", "sync", 2),
        ("prkr", "sync", 2), ("cach", "sync", 2), ("xcp", "sync", 2),
        ("zp", "sync", 2), ("dtp", "sync", 2), ("xcbr", "sync", 2),
        ("sdt", "sync", 2), ("sxc", "sync", 2), ("zl", "sync", 2),
        ("pop", "sync", 2), ("e1w", "sync", 1), ("e1r", "sync", 1),
        ("e2w", "sync", 1), ("e3w", "sync", 1), ("outl", "sync", 2),
        ("outs", "sync", 2),
        ("p1l", "scalar", 2), ("p2l", "scalar", 2), ("wgl", "scalar", 2),
        ("wdn", "scalar", 4),
        ("winl", "gpsimd", 2), ("wxl", "gpsimd", 1), ("wdtl", "gpsimd", 1),
        ("wol", "gpsimd", 2), ("barw", "gpsimd", 1), ("miw", "gpsimd", 1),
        ("mir", "gpsimd", 1), ("e2r", "gpsimd", 1), ("e3r", "gpsimd", 1),
    ):
        ctx.stream(nm, eng, wd)
    for i in range(9):
        ctx.csem(f"ccb{i}")

    import contextlib
    with (
        nc.psum_tensor("ps0", [128, 1024], F32) as ps0,
        nc.psum_tensor("ps1", [128, 1024], F32) as ps1,
        nc.psum_tensor("ps2", [128, 1024], F32) as ps2,
        nc.psum_tensor("ps3", [128, 1024], F32) as ps3,
        contextlib.ExitStack() as _stk,
    ):
        PS = [ps0, ps1, ps2, ps3]
        sems = {}
        bar_i = [0]

        def barrier(w=(), pairs=True):
            i = bar_i[0]
            bar_i[0] += 1
            assert i < NBAR
            rg = [[0, 1], [2, 3], [4, 5], [6, 7]] if pairs else [list(range(NC))]
            bw = ctx.dma("barw", lambda e, S, i=i: e.dma_start(
                out=cc_bi[i][:, :], in_=ccsb[0:1, 0:8]),
                w=list(w) + [ctx.tok("v")])
            return ctx.emit("gpsimd", lambda e, S, i=i, rg=rg:
                            e.collective_compute(
                "AllReduce", OP.add, replica_groups=rg,
                ins=[cc_bi[i].ap().opt()], outs=[cc_bo[i].ap().opt()]),
                w=[bw], inc=f"ccb{i}", k=1)

        r_mu, r_e2, r_sd, r_am, r_rs, r_mrs, r_dq, r_t = [
            (i * T, (i + 1) * T) for i in range(8)]

        def row(rr):
            return rowb[0:1, rr[0]:rr[1]]

        ldc = []
        for dst, src in ((id128, d_id), (h128, d_h128), (ones_c, d_ones),
                         (ones_cb, d_onesb), (ones_rf, d_onerf), (eall, d_eall),
                         (cwt, d_cw), (cbt, d_cb), (ddt, d_dd), (bdtt, d_bdt),
                         (alt, d_al)):
            ldc.append(ctx.dma("cst", lambda e, S, dst=dst, src=src:
                                e.dma_start(out=dst[:, :], in_=src[:, :])))
        tk = ctx.s(lambda e, S: e.activation(out=acol[:, :], in_=alt[:, :],
                                             func=AF.Exp, bias=0.0, scale=1.0),
                   w=[ldc[-1]])
        ctx.v(lambda e, S: e.tensor_scalar(out=acol[:, :], in0=acol[:, :],
                                           scalar1=-1.0, scalar2=None,
                                           op0=OP.mult), w=[tk])
        ctx.v(lambda e, S: e.memset(ccsb[:, :], 0.0))
        ctx.v(lambda e, S: e.memset(states[:, :], 0.0))

        ldx = []
        for tc in range(8):
            ldx.append(ctx.dma("xin", lambda e, S, tc=tc: e.dma_start(
                out=arb_f[:, tc * 2048:(tc + 1) * 2048],
                in_=d_x0[tc * 128:(tc + 1) * 128, :])))
        cp = []
        for tc in range(8):
            for fc in range(NF):
                k = tc * NF + fc
                pw = [ldc[0], ldx[tc]] + ([cp[k - 2]] if k >= 2 else [])
                tt = ctx.t(lambda e, S, tc=tc, fc=fc, k=k: e.transpose(
                    out=PS[k % 2][:, 0:128],
                    in_=arb_f[:, tc * 2048 + fc * 128: tc * 2048 + fc * 128 + 128],
                    identity=id128[:, :]), w=pw)
                cp.append(ctx.v(lambda e, S, tc=tc, fc=fc, k=k: e.tensor_copy(
                    out=xtile(fc)[:, tc * 128: tc * 128 + 128],
                    in_=PS[k % 2][:, 0:128]), w=[tt]))
        tok_x_in = cp[-1]

        a0 = ctx.v(lambda e, S: e.memset(p1st[:, 4096:4104], 0.0),
                   w=[tok_x_in])
        prev_r = a0
        ld_sl = [None, None]
        for ci in range(24):
            mi = ci // 8
            sl = ci % 2
            fo = sl * 2048
            pw = [tok_x_in] + ([ld_sl[sl]] if ld_sl[sl] else [])
            ld = ctx.dma("p1l", lambda e, S, ci=ci, fo=fo: e.dma_start(
                out=p1st[:, fo:fo + 2048],
                in_=d_p1[:, ci * 2048:(ci + 1) * 2048]), w=pw)
            rd = ctx.v(lambda e, S, fo=fo: e.tensor_reduce(
                out=p1st[:, 4100:4101], in_=p1st[:, fo:fo + 2048], axis=AX.X,
                op=OP.add, apply_absolute_value=True), w=[ld, prev_r])
            prev_r = ctx.v(lambda e, S, mi=mi: e.tensor_tensor(
                out=p1st[:, 4096 + mi:4097 + mi],
                in0=p1st[:, 4096 + mi:4097 + mi],
                in1=p1st[:, 4100:4101], op=OP.add), w=[rd])
            ld_sl[sl] = rd
        tsum = ctx.t(lambda e, S: e.matmul(
            PS[2][0:1, 0:3], ones_c[:, 0:1], p1st[:, 4096:4099],
            start=True, stop=True), w=[prev_r, tok_x_in])
        cps = ctx.v(lambda e, S: e.tensor_copy(out=ccsb[0:1, 0:3],
                                               in_=PS[2][0:1, 0:3]), w=[tsum])
        miw = ctx.dma("miw", lambda e, S: e.dma_start(
            out=cc_mi[:, :], in_=ccsb[0:1, 0:8]), w=[cps])
        cct = ctx.emit("gpsimd", lambda e, S: e.collective_compute(
            "AllReduce", OP.add, replica_groups=[list(range(NC))],
            ins=[cc_mi.ap().opt()], outs=[cc_mo.ap().opt()]), w=[miw],
            inc="ccb8", k=1)
        ldmo = ctx.dma("mir", lambda e, S: e.dma_start(
            out=ccsb[0:1, 8:16], in_=cc_mo[:, :]), w=[cct])
        NW = 1.0 / float(II) / float(H)
        tk = ctx.v(lambda e, S: e.tensor_scalar(
            out=ccsb[0:1, 8:11], in0=ccsb[0:1, 8:11], scalar1=NW,
            scalar2=EPS_Q, op0=OP.mult, op1=OP.max), w=[ldmo])
        tk = ctx.v(lambda e, S: e.reciprocal(out=ccsb[0:1, 12:15],
                                             in_=ccsb[0:1, 8:11]), w=[tk])
        tb = ctx.t(lambda e, S: e.matmul(PS[2][:, 0:8], ones_rf[0:1, :],
                                         ccsb[0:1, 8:16], start=True,
                                         stop=True), w=[tk, cps])
        tkw = ctx.v(lambda e, S: e.tensor_copy(out=wsc[:, 0:3],
                                               in_=PS[2][:, 4:7]), w=[tb])
        tok_wsc = ctx.v(lambda e, S: e.tensor_copy(out=wsc[:, 3:6],
                                                   in_=PS[2][:, 0:3]),
                        w=[tb, tkw])

        st_tok = [None, None]
        wq_tok = [None, None]
        wq_map = {}
        chunks = []
        for b in range(64):
            chunks.append((0, d_wg, wq_g, b))
            chunks.append((1, d_wu, wq_u, b))
        chunks += [(2, d_wd, wq_d, b) for b in range(64)]
        for ci, (mi, src, dst, b) in enumerate(chunks):
            sl = ci % 2
            fo = sl * 2048
            bo = sl * 2048
            pw = [tok_wsc] + ([st_tok[sl]] if st_tok[sl] else [])
            ld = ctx.dma("p2l", lambda e, S, src=src, b=b, fo=fo:
                         e.dma_start(
                out=p2st[:, fo:fo + 2048],
                in_=src[:, b * 2048:(b + 1) * 2048]), w=pw)
            q1 = ctx.v(lambda e, S, fo=fo, mi=mi: e.tensor_scalar(
                out=p2st[:, fo:fo + 2048], in0=p2st[:, fo:fo + 2048],
                scalar1=wsc[:, mi:mi + 1], scalar2=MAGIC, op0=OP.mult,
                op1=OP.add), w=[ld])
            q2 = ctx.v(lambda e, S, fo=fo: e.tensor_scalar(
                out=p2st[:, fo:fo + 2048], in0=p2st[:, fo:fo + 2048],
                scalar1=MAGIC, scalar2=1.0, op0=OP.subtract, op1=OP.min),
                w=[q1])
            pw3 = [q2] + ([wq_tok[sl]] if wq_tok[sl] else [])
            q3 = ctx.v(lambda e, S, fo=fo, bo=bo: e.tensor_scalar(
                out=p2wq[:, bo:bo + 2048], in0=p2st[:, fo:fo + 2048],
                scalar1=-1.0, scalar2=None, op0=OP.max), w=pw3)
            st_tok[sl] = q3
            wq_tok[sl] = wq_map[(mi, b)] = ctx.dma(
                "cach", lambda e, S, dst=dst, b=b, bo=bo: e.dma_start(
                    out=dst[:, b * 2048:(b + 1) * 2048],
                    in_=p2wq[:, bo:bo + 2048]), w=[q3])
        tok_cache = ctx.fence("cach")

        def ln_stats(w_entry, cst):
            stat = []
            for k in range(NF):
                c = ctx.v(lambda e, S, k=k: e.tensor_copy(
                    out=cst[:, (k % 2) * 1024:(k % 2) * 1024 + 1024],
                    in_=xtile(k)),
                    w=[w_entry] + ([stat[2 * k - 4]] if k >= 2 else []))
                sq = ctx.s(lambda e, S, k=k: e.activation(
                    out=cst[:, 2048 + (k % 2) * 1024:
                            2048 + (k % 2) * 1024 + 1024],
                    in_=xtile(k), func=AF.Square,
                    bias=0.0, scale=1.0),
                    w=[w_entry] + ([stat[2 * k - 3]] if k >= 2 else []))
                m1 = m2 = None
                for h2 in range(2):
                    m1 = ctx.t(lambda e, S, k=k, h2=h2: e.matmul(
                        PS[0][0:1, h2 * 512:h2 * 512 + 512], ones_cb[:, 0:1],
                        cst[:, (k % 2) * 1024 + h2 * 512:
                            (k % 2) * 1024 + h2 * 512 + 512],
                        start=(k == 0), stop=(k == NF - 1)), w=[c])
                    m2 = ctx.t(lambda e, S, k=k, h2=h2: e.matmul(
                        PS[1][0:1, h2 * 512:h2 * 512 + 512], ones_cb[:, 0:1],
                        cst[:, 2048 + (k % 2) * 1024 + h2 * 512:
                            2048 + (k % 2) * 1024 + h2 * 512 + 512],
                        start=(k == 0), stop=(k == NF - 1)), w=[sq])
                stat.extend([m1, m2])
            tstat = stat[-1]
            tk = ctx.v(lambda e, S: e.tensor_scalar(
                out=row(r_mu), in0=PS[0][0:1, :], scalar1=INV_H, scalar2=None,
                op0=OP.mult), w=[tstat])
            tk = ctx.v(lambda e, S: e.tensor_scalar(
                out=row(r_e2), in0=PS[1][0:1, :], scalar1=INV_H, scalar2=None,
                op0=OP.mult), w=[tstat, tk])
            tk = ctx.v(lambda e, S: e.tensor_tensor(
                out=row(r_t), in0=row(r_mu), in1=row(r_mu), op=OP.mult),
                w=[tk])
            tk = ctx.v(lambda e, S: e.tensor_tensor(
                out=row(r_e2), in0=row(r_e2), in1=row(r_t), op=OP.subtract),
                w=[tk])
            tk = ctx.v(lambda e, S: e.tensor_scalar(
                out=row(r_e2), in0=row(r_e2), scalar1=EPS_LN, scalar2=None,
                op0=OP.add), w=[tk])
            ts_ = ctx.s(lambda e, S: e.activation(
                out=row(r_sd), in_=row(r_e2), func=AF.Sqrt, bias=0.0,
                scale=1.0), w=[tk])
            tk = ctx.v(lambda e, S: e.reciprocal(out=row(r_sd),
                                                 in_=row(r_sd)), w=[ts_])
            tk = ctx.v(lambda e, S: e.tensor_tensor(
                out=row(r_t), in0=row(r_sd), in1=row(r_sd), op=OP.mult),
                w=[tk])
            tk = ctx.v(lambda e, S: e.tensor_tensor(
                out=row(r_t), in0=row(r_t), in1=row(r_e2), op=OP.mult),
                w=[tk])
            tk = ctx.v(lambda e, S: e.tensor_scalar(
                out=row(r_t), in0=row(r_t), scalar1=-0.5, scalar2=1.5,
                op0=OP.mult, op1=OP.add), w=[tk])
            tok_rows = ctx.v(lambda e, S: e.tensor_tensor(
                out=row(r_sd), in0=row(r_sd), in1=row(r_t), op=OP.mult),
                w=[tk])
            return tok_rows, tstat

        def bcast(ps_i, rr, w):
            tb_ = None
            for h2 in range(2):
                tb_ = ctx.t(lambda e, S, h2=h2, ps_i=ps_i, rr=rr: e.matmul(
                    PS[ps_i][:, h2 * 512:h2 * 512 + 512], ones_rf[0:1, :],
                    rowb[0:1, rr[0] + h2 * 512: rr[0] + h2 * 512 + 512],
                    start=True, stop=True), w=w)
            return tb_

        def amax_to_row(src_fn, w):
            prev = None
            for tc8 in range(8):
                tt = ctx.t(lambda e, S, tc8=tc8: e.transpose(
                    out=PS[3][:, 0:128], in_=src_fn(tc8),
                    identity=id128[:, :]), w=list(w) + ([prev] if prev else []))
                rv = ctx.v(lambda e, S, tc8=tc8: e.tensor_reduce(
                    out=colb[:, tc8:tc8 + 1], in_=PS[3][:, 0:128], axis=AX.X,
                    op=OP.max), w=[tt])
                tt2 = ctx.t(lambda e, S, tc8=tc8: e.transpose(
                    out=PS[3][0:1, 128:256], in_=colb[:, tc8:tc8 + 1],
                    identity=id128[:, :]), w=[rv])
                prev = ctx.v(lambda e, S, tc8=tc8: e.tensor_copy(
                    out=rowb[0:1, r_am[0] + tc8 * 128:
                             r_am[0] + tc8 * 128 + 128],
                    in_=PS[3][0:1, 128:256]), w=[tt2])
            return prev

        def emit_ffn(dst_park, w_entry):
            tok_rows, tstat = ln_stats(w_entry, lncast)
            tmu = bcast(2, r_mu, [tok_rows, tstat])
            df = [dff[:, 0:1024], dff[:, 1024:2048]]
            mt = mtt[:, 0:1024]
            mtn = mtt2[:, 0:1024]
            prev = None
            for k in range(NF):
                d = df[k % 2]
                t1 = ctx.v(lambda e, S, k=k, d=d: e.tensor_tensor(
                    out=d, in0=xtile(k), in1=PS[2][:, :],
                    op=OP.subtract), w=[tmu] + ([prev] if prev else []))
                if k == 0:
                    prev = ctx.v(lambda e, S: e.tensor_copy(
                        out=mt, in_=df[0]), w=[t1])
                    prev = ctx.v(lambda e, S: e.tensor_copy(
                        out=mtn, in_=df[0]), w=[t1, prev])
                else:
                    prev = ctx.v(lambda e, S, d=d: e.tensor_tensor(
                        out=mt, in0=mt, in1=d, op=OP.max), w=[t1, prev])
                    prev = ctx.v(lambda e, S, d=d: e.tensor_tensor(
                        out=mtn, in0=mtn, in1=d, op=OP.min), w=[t1, prev])
            prev = ctx.v(lambda e, S: e.tensor_scalar(
                out=mtn, in0=mtn, scalar1=-1.0, scalar2=None, op0=OP.mult),
                w=[prev])
            prev = ctx.v(lambda e, S: e.tensor_tensor(
                out=mt, in0=mt, in1=mtn, op=OP.max), w=[prev])
            pra = amax_to_row(lambda tc8: mt[:, tc8 * 128:tc8 * 128 + 128],
                              [prev])
            tk = ctx.v(lambda e, S: e.tensor_tensor(
                out=row(r_am), in0=row(r_am), in1=row(r_sd), op=OP.mult),
                w=[pra, tok_rows])
            tk = ctx.v(lambda e, S: e.tensor_scalar(
                out=row(r_am), in0=row(r_am), scalar1=EPS_Q, scalar2=None,
                op0=OP.max), w=[tk])
            tk2 = ctx.v(lambda e, S: e.reciprocal(out=row(r_rs),
                                                  in_=row(r_am)), w=[tk])
            tk2 = ctx.v(lambda e, S: e.tensor_scalar(
                out=row(r_rs), in0=row(r_rs), scalar1=128.0, scalar2=None,
                op0=OP.mult), w=[tk2])
            tk2 = ctx.v(lambda e, S: e.tensor_tensor(
                out=row(r_rs), in0=row(r_rs), in1=row(r_sd), op=OP.mult),
                w=[tk2])
            tk3 = ctx.v(lambda e, S: e.tensor_tensor(
                out=row(r_mrs), in0=row(r_mu), in1=row(r_rs), op=OP.mult),
                w=[tk2])
            tk4 = ctx.v(lambda e, S: e.tensor_scalar(
                out=row(r_dq), in0=row(r_am), scalar1=1.0 / 128.0,
                scalar2=None, op0=OP.mult), w=[tk])
            tb0 = bcast(0, r_rs, [tk2, ctx.tok("v")])
            tb1 = bcast(1, r_mrs, [tk3])
            tb2 = bcast(2, r_dq, [tk4, ctx.tok("v")])
            parks = []
            qts = []
            for k in range(NF):
                d = df[k % 2]
                t1 = ctx.v(lambda e, S, k=k, d=d: e.tensor_tensor(
                    out=d, in0=xtile(k), in1=PS[0][:, :],
                    op=OP.mult), w=[tb0, tb1, tb2] +
                    ([qts[k - 2]] if k >= 2 else []))
                parks.append(ctx.dma("prkw", lambda e, S, k=k: e.dma_start(
                    out=dst_park[:, k * T:(k + 1) * T],
                    in_=xtile(k)), w=[t1]))
                t1 = ctx.v(lambda e, S, d=d: e.tensor_tensor(
                    out=d, in0=d, in1=PS[1][:, :], op=OP.subtract), w=[t1])
                t1 = ctx.v(lambda e, S, d=d: e.tensor_scalar(
                    out=d, in0=d, scalar1=MAGIC, scalar2=MAGIC, op0=OP.add,
                    op1=OP.subtract), w=[t1])
                t1 = ctx.v(lambda e, S, d=d: e.tensor_scalar(
                    out=d, in0=d, scalar1=127.0, scalar2=-128.0, op0=OP.min,
                    op1=OP.max), w=[t1])
                t1 = ctx.v(lambda e, S, k=k, d=d: e.tensor_tensor(
                    out=xq_t[:, k * T:(k + 1) * T], in0=d, in1=PS[2][:, :],
                    op=OP.mult), w=[t1, parks[k // 2]])
                qts.append(t1)
            tok_xq = qts[-1]

            xcopy = {}
            for m in range(64):
                p = m % 2
                gps, ups = PS[2 * p], PS[2 * p + 1]
                pw = [tok_xq] + ([xcopy[m - 2]] if m >= 2 else [])
                lg = ctx.dma("wgl", lambda e, S, m=m, p=p: e.dma_start(
                    out=wgu[:, p * 2048:p * 2048 + 2048],
                    in_=wq_g[:, m * 2048:(m + 1) * 2048]),
                    w=pw + [wq_map[(0, m)]])
                lu = ctx.dma("wgl", lambda e, S, m=m, p=p: e.dma_start(
                    out=wgu[:, 4096 + p * 2048:4096 + p * 2048 + 2048],
                    in_=wq_u[:, m * 2048:(m + 1) * 2048]),
                    w=pw + [wq_map[(1, m)]])
                mm = None
                for h2 in range(2):
                    for k in range(NF):
                        fn = (lambda e, S, p=p, h2=h2, k=k, gps=gps: e.matmul(
                            gps[:, h2 * 512:h2 * 512 + 512],
                            wgu[:, p * 2048 + k * 128:
                                p * 2048 + k * 128 + 128],
                            xq_t[:, k * T + h2 * 512: k * T + h2 * 512 + 512],
                            start=(k == 0), stop=(k == NF - 1)))
                        wv = [lg, tok_xq] + ([xcopy[m - 2]] if m >= 2 else [])
                        mm = ctx.t(fn, wv) if (h2 == 1 and k == NF - 1) \
                            else ctx.tq(fn, wv)
                for h2 in range(2):
                    for k in range(NF):
                        fn = (lambda e, S, p=p, h2=h2, k=k, ups=ups: e.matmul(
                            ups[:, h2 * 512:h2 * 512 + 512],
                            wgu[:, 4096 + p * 2048 + k * 128:
                                4096 + p * 2048 + k * 128 + 128],
                            xq_t[:, k * T + h2 * 512: k * T + h2 * 512 + 512],
                            start=(k == 0), stop=(k == NF - 1)))
                        mm = ctx.t(fn, [lu]) if (h2 == 1 and k == NF - 1) \
                            else ctx.tq(fn, [lu])
                sg = ctx.s(lambda e, S, p=p, gps=gps: e.activation(
                    out=vgu[:, p * 1024:p * 1024 + 1024],
                    in_=gps[:, :], func=AF.Silu, bias=0.0,
                    scale=wsc[:, 3:4]), w=[mm, tok_wsc])
                su = ctx.s(lambda e, S, p=p, ups=ups: e.mul(
                    out=vgu[:, 2048 + p * 1024:2048 + p * 1024 + 1024],
                    in_=ups[:, :], mul=wsc[:, 4:5]), w=[mm])
                vv = ctx.v(lambda e, S, p=p: e.tensor_tensor(
                    out=vgu[:, 4096 + p * 1024:4096 + p * 1024 + 1024],
                    in0=vgu[:, p * 1024:p * 1024 + 1024],
                    in1=vgu[:, 2048 + p * 1024:2048 + p * 1024 + 1024],
                    op=OP.mult), w=[sg, su])
                sb = None
                for h2 in range(2):
                    sb = ctx.t(lambda e, S, p=p, h2=h2, gps=gps: e.matmul(
                        gps[:, h2 * 512:h2 * 512 + 512], h128[:, :],
                        vgu[:, 4096 + p * 1024 + h2 * 512:
                            4096 + p * 1024 + h2 * 512 + 512],
                        start=True, stop=True), w=[vv])
                xcopy[m] = ctx.v(lambda e, S, m=m, gps=gps: e.tensor_copy(
                    out=arb_b[:, m * T:(m + 1) * T], in_=gps[:, :]),
                    w=[sb] + (ctx.fence("prkw") if m >= 48 else []) +
                    (ctx.fence("cach") if 39 <= m <= 45 else []))
            tok_gu = xcopy[63]

            SPARE = -1
            slot = list(range(64))
            spare_slot = SPARE

            def tap(sl):
                if sl == SPARE:
                    return spare_t[:, 0:1024]
                return arb_b[:, sl * T:(sl + 1) * T]
            prev_op = tok_gu
            for st in range(6):
                dist = 1 << st
                for grp in range(0, 64, 2 * dist):
                    for j in range(dist):
                        lo, hi = grp + j, grp + j + dist
                        slo, shi, ssp = slot[lo], slot[hi], spare_slot
                        prev_op = ctx.v(lambda e, S, slo=slo, shi=shi,
                                        ssp=ssp: e.tensor_tensor(
                            out=tap(ssp), in0=tap(slo), in1=tap(shi),
                            op=OP.subtract), w=[prev_op])
                        prev_op = ctx.v(lambda e, S, slo=slo,
                                        shi=shi: e.tensor_tensor(
                            out=tap(slo), in0=tap(slo), in1=tap(shi),
                            op=OP.add), w=[prev_op])
                        slot[hi] = ssp
                        spare_slot = shi
            tok_fa = prev_op

            mb = mbb[:, 0:1024]
            mbn = vgu[:, 0:1024]
            prev = None
            for k in range(64):
                sl = slot[k]
                if k == 0:
                    prev = ctx.v(lambda e, S, sl=sl: e.tensor_copy(
                        out=mb, in_=tap(sl)), w=[tok_fa])
                    prev = ctx.v(lambda e, S, sl=sl: e.tensor_copy(
                        out=mbn, in_=tap(sl)), w=[tok_fa, prev])
                else:
                    prev = ctx.v(lambda e, S, sl=sl: e.tensor_tensor(
                        out=mb, in0=mb, in1=tap(sl), op=OP.max), w=[prev])
                    prev = ctx.v(lambda e, S, sl=sl: e.tensor_tensor(
                        out=mbn, in0=mbn, in1=tap(sl), op=OP.min), w=[prev])
            mf = mff[:, 0:1024]
            prev = ctx.v(lambda e, S: e.tensor_scalar(
                out=mbn, in0=mbn, scalar1=-1.0, scalar2=None, op0=OP.mult),
                w=[prev])
            prev = ctx.v(lambda e, S: e.tensor_tensor(
                out=mb, in0=mb, in1=mbn, op=OP.max), w=[prev])
            prev = ctx.v(lambda e, S: e.tensor_copy(out=mf, in_=mb), w=[prev])
            pra = amax_to_row(lambda tc8: mf[:, tc8 * 128:tc8 * 128 + 128],
                              [prev])
            tk = ctx.v(lambda e, S: e.tensor_scalar(
                out=row(r_am), in0=row(r_am), scalar1=EPS_Q, scalar2=None,
                op0=OP.max), w=[pra])
            tk2 = ctx.v(lambda e, S: e.reciprocal(out=row(r_rs),
                                                  in_=row(r_am)), w=[tk])
            tk2 = ctx.v(lambda e, S: e.tensor_scalar(
                out=row(r_rs), in0=row(r_rs), scalar1=128.0, scalar2=None,
                op0=OP.mult), w=[tk2])
            tk4 = ctx.v(lambda e, S: e.tensor_scalar(
                out=row(r_dq), in0=row(r_am), scalar1=(0.5 * ISQ_I / 128.0),
                scalar2=ccsb[0:1, 10:11], op0=OP.mult, op1=OP.mult),
                w=[tk, tok_wsc])
            tb0 = bcast(0, r_rs, [tk2, ctx.tok("v")])
            tb2 = bcast(2, r_dq, [tk4, ctx.tok("v")])
            tb2c = ctx.v(lambda e, S: e.tensor_copy(out=scomb[:, 0:1024],
                                                    in_=PS[2][:, :]), w=[tb2])
            q8 = []
            for k in range(64):
                sl = slot[k]
                tf = qtf[:, (k % 2) * 1024:(k % 2) * 1024 + 1024]
                t1 = ctx.v(lambda e, S, sl=sl, tf=tf: e.tensor_tensor(
                    out=tf, in0=tap(sl), in1=PS[0][:, :], op=OP.mult),
                    w=[tb0, tb2] + ([q8[k - 2]] if k >= 2 else []))
                t1 = ctx.v(lambda e, S, tf=tf: e.tensor_scalar(
                    out=tf, in0=tf, scalar1=MAGIC, scalar2=MAGIC, op0=OP.add,
                    op1=OP.subtract), w=[t1])
                t1 = ctx.v(lambda e, S, sl=sl, tf=tf: e.tensor_scalar(
                    out=tap(sl), in0=tf, scalar1=127.0, scalar2=-128.0,
                    op0=OP.min, op1=OP.max), w=[t1])
                q8.append(t1)
            tok_q8 = q8[-1]

            dn = []
            last_park = None
            for m in range(16):
                p = m % 2
                dps = PS[1] if p else PS[3]
                wdt_ = {}
                for hb in range(2):
                    pw = [tok_q8, wq_map[(2, 4 * m + 2 * hb)],
                          wq_map[(2, 4 * m + 2 * hb + 1)]] + \
                        ([dn[m - 2]] if m >= 2 else [])
                    wdt_[hb] = ctx.dma("wdn", lambda e, S, m=m, hb=hb:
                                       e.dma_start(
                        out=wdh[:, (m % 2) * 8192 + hb * 4096:
                                (m % 2) * 8192 + hb * 4096 + 4096],
                        in_=wq_d[:, m * 8192 + hb * 4096:
                                 m * 8192 + hb * 4096 + 4096]), w=pw)
                mm = None
                for hb in range(2):
                    for kk in range(32):
                        k = hb * 32 + kk
                        sl = slot[k]
                        for h2 in range(2):
                            fn = (lambda e, S, m=m, hb=hb, kk=kk, k=k, sl=sl,
                                  h2=h2, dps=dps: e.matmul(
                                dps[:, h2 * 512:h2 * 512 + 512],
                                wdh[:, (m % 2) * 8192 + hb * 4096 + kk * 128:
                                    (m % 2) * 8192 + hb * 4096 + kk * 128 + 128],
                                tap(sl)[:, h2 * 512:h2 * 512 + 512],
                                start=(k == 0), stop=(k == 63)))
                            wv = [wdt_[hb], tok_q8] + \
                                ([dn[m - 2]] if m >= 2 else [])
                            mm = ctx.t(fn, wv) if (k == 63 and h2 == 1) \
                                else ctx.tq(fn, wv)
                xs = xst[:, p * 1024:p * 1024 + 1024]
                es = est[:, 0:1024]
                lx = ctx.dma("prkr", lambda e, S, m=m, xs=xs: e.dma_start(
                    out=xs, in_=dst_park[:, m * T:(m + 1) * T]), w=[mm])
                ev = ctx.v(lambda e, S, es=es, dps=dps: e.tensor_tensor(
                    out=es, in0=dps[:, :], in1=scomb[:, 0:1024], op=OP.mult),
                    w=[mm, tb2c] + ([last_park] if last_park else []))
                ev = ctx.v(lambda e, S, es=es, xs=xs: e.tensor_tensor(
                    out=es, in0=es, in1=xs, op=OP.add), w=[ev, lx])
                dn.append(ev)
                last_park = ctx.dma("prkw", lambda e, S, m=m, es=es:
                                    e.dma_start(
                    out=dst_park[:, m * T:(m + 1) * T], in_=es), w=[ev])
            return last_park

        def emit_mamba(w_entry):
            import os as _os
            tok_rows, tstat = ln_stats(w_entry, mlncast)
            tkm = ctx.v(lambda e, S: e.tensor_tensor(
                out=row(r_mrs), in0=row(r_mu), in1=row(r_sd), op=OP.mult),
                w=[tok_rows])
            tbr = bcast(0, r_sd, [tok_rows, tstat])
            tbm = bcast(1, r_mrs, [tkm])
            df = [mdff[:, 0:1024], mdff[:, 1024:2048]]
            xls = []
            for k in range(NF):
                d = df[k % 2]
                t1 = ctx.v(lambda e, S, k=k, d=d: e.tensor_tensor(
                    out=d, in0=xtile(k), in1=PS[0][:, :],
                    op=OP.mult), w=[tbr, tbm] +
                    ([xls[k - 2]] if k >= 2 else []))
                t1 = ctx.v(lambda e, S, k=k, d=d: e.tensor_tensor(
                    out=arb_b[:, k * 2048:k * 2048 + 1024], in0=d,
                    in1=PS[1][:, :], op=OP.subtract), w=[t1])
                xls.append(t1)
            xln_v = arb_b.ap().rearrange("c (k t) -> c k t", k=32)

            def e1w(e, S):
                for arm in e.switch_core_id(n=ARMN):
                    hh = int(arm) % 2
                    e.dma_start(out=s_xln[hh, :, :],
                                in_=xln_v[:, 0:16, 0:1024]).then_inc(
                        S["e1w0"], 16)
                return None
            t_e1w = ctx.emit("sync", e1w, w=xls, inc="e1w0", k=16)
            b1 = barrier(w=[t_e1w], pairs=True)
            ctx.dma("e1r", lambda e, S: e.dma_start(
                out=xln_v[:, 0:16, 0:1024], in_=s_xln[0, :, :]), w=[b1])
            tok_xln = ctx.dma("e1r", lambda e, S: e.dma_start(
                out=xln_v[:, 0:16, 1024:2048], in_=s_xln[1, :, :]), w=[b1])

            if msub < 2:
                return None
            win_tok = {}
            xz_mm = []
            ps_last = {0: None, 1: None}
            conv_last = [None]
            silu_last = [None]
            xcpk = {}
            zpk = {}
            for m in range(32):
                p = m % 2
                pw = [tok_xln] + ([win_tok[m - 2]] if m >= 2 else []) + \
                    ([xz_mm[-1]] if xz_mm else [])
                win_tok[m] = ctx.dma("winl", lambda e, S, m=m, p=p:
                                     e.dma_start(
                    out=winb[:, p * 2048:p * 2048 + 2048],
                    in_=d_win[:, m * 2048:(m + 1) * 2048]), w=pw)
                ch_toks = []
                for ch in range(4):
                    q = (m * 4 + ch) % 2
                    mm = None
                    for k in range(NF):
                        fn = (lambda e, S, p=p, ch=ch, k=k, q=q: e.matmul(
                            PS[q][:, 0:512],
                            winb[:, p * 2048 + k * 128:
                                 p * 2048 + k * 128 + 128],
                            arb_b[:, k * 2048 + ch * 512:
                                  k * 2048 + ch * 512 + 512],
                            start=(k == 0), stop=(k == NF - 1)))
                        wv = [win_tok[m], tok_xln] + \
                            ([ps_last[q]] if ps_last[q] else [])
                        mm = ctx.t(fn, wv) if k == NF - 1 else ctx.tq(fn, wv)
                    xz_mm.append(mm)
                    if m < 16:
                        ev = ctx.v(lambda e, S, ch=ch, q=q: e.tensor_copy(
                            out=xpscr[:, ch * 512:ch * 512 + 512],
                            in_=PS[q][:, 0:512]), w=[mm, conv_last[0]])
                    else:
                        ev = ctx.s(lambda e, S, ch=ch, q=q: e.activation(
                            out=zstg[:, ch * 512:ch * 512 + 512],
                            in_=PS[q][:, 0:512], func=AF.Silu, bias=0.0,
                            scale=1.0),
                            w=[mm] + ([zpk[m - 1]] if (m >= 17 and ch == 0)
                                      else []))
                    ch_toks.append(ev)
                    ps_last[q] = ev
                if m < 16:
                    c1 = ctx.v(lambda e, S, m=m: e.tensor_scalar(
                        out=cvscr[:, 0:2048], in0=xpscr[:, 0:2048],
                        scalar1=cwt[:, 3 * m + 1:3 * m + 2], scalar2=None,
                        op0=OP.mult), w=ch_toks + [silu_last[0]])
                    c1 = ctx.v(lambda e, S, m=m: e.scalar_tensor_tensor(
                        out=cvscr[:, 1:2048], in0=xpscr[:, 0:2047],
                        scalar=cwt[:, 3 * m:3 * m + 1],
                        in1=cvscr[:, 1:2048], op0=OP.mult, op1=OP.add),
                        w=[c1])
                    c1 = ctx.v(lambda e, S, m=m: e.scalar_tensor_tensor(
                        out=cvscr[:, 0:2047], in0=xpscr[:, 1:2048],
                        scalar=cwt[:, 3 * m + 2:3 * m + 3],
                        in1=cvscr[:, 0:2047], op0=OP.mult, op1=OP.add),
                        w=[c1])
                    conv_last[0] = c1
                    sl_ = silu_last[0] = ctx.s(lambda e, S, m=m, p=p: e.activation(
                        out=xcstg[:, p * 2048:p * 2048 + 2048],
                        in_=cvscr[:, 0:2048], func=AF.Silu,
                        bias=cbt[:, m:m + 1], scale=1.0),
                        w=[c1] + ([xcpk[m - 2]] if m >= 2 else []))
                    xcpk[m] = ctx.dma("xcp", lambda e, S, m=m, p=p:
                                      e.dma_start(
                        out=p_xc[:, m * 2048:(m + 1) * 2048],
                        in_=xcstg[:, p * 2048:p * 2048 + 2048]), w=[sl_])
                else:
                    zpk[m] = ctx.dma("zp", lambda e, S, m=m: e.dma_start(
                        out=p_z[:, (m - 16) * 2048:(m - 15) * 2048],
                        in_=zstg[:, 0:2048]), w=ch_toks)
            tok_xc = ctx.fence("xcp")

            if msub < 3:
                return None
            lwx = ctx.dma("wxl", lambda e, S: e.dma_start(
                out=wxs[:, 0:2560], in_=d_wx[:, :]),
                w=[win_tok[31], ctx.tok("s")])
            xcbt = {}
            mmd = None
            for k in range(NF):
                p = k % 2
                lk = ctx.dma("xcbr", lambda e, S, k=k, p=p: e.dma_start(
                    out=xcb[:, p * 2048:p * 2048 + 2048],
                    in_=p_xc[:, k * 2048:(k + 1) * 2048]),
                    w=tok_xc + [lwx] + ([xcbt[k - 2]] if k >= 2 else []))
                for nch in range(4):
                    fn = (lambda e, S, k=k, p=p, nch=nch: e.matmul(
                        PS[nch // 2][:, (nch % 2) * 512:(nch % 2) * 512 + 512],
                        wxs[:, k * 160: k * 160 + 128],
                        xcb[:, p * 2048 + nch * 512:
                            p * 2048 + nch * 512 + 512],
                        start=(k == 0), stop=(k == NF - 1)))
                    mmd = ctx.t(fn, [lk, lwx]) if (k == NF - 1 and nch == 3) \
                        else ctx.tq(fn, [lk, lwx])
                    fn2 = (lambda e, S, k=k, p=p, nch=nch: e.matmul(
                        PS[2 + nch // 2][0:32,
                                         (nch % 2) * 512:(nch % 2) * 512 + 512],
                        wxs[:, k * 160 + 128: k * 160 + 160],
                        xcb[:, p * 2048 + nch * 512:
                            p * 2048 + nch * 512 + 512],
                        start=(k == 0), stop=(k == NF - 1)))
                    mmd = ctx.t(fn2, [lk, lwx]) if nch == 3 \
                        else ctx.tq(fn2, [lk, lwx])
                xcbt[k] = mmd
            eva = ctx.v(lambda e, S: e.tensor_copy(
                out=xdbla[:, 0:1024], in_=PS[0][:, :]), w=[mmd])
            eva = ctx.v(lambda e, S: e.tensor_copy(
                out=xdbla[:, 1024:2048], in_=PS[1][:, :]), w=[mmd, eva])
            evb = ctx.v(lambda e, S: e.tensor_copy(
                out=t32f[:, 0:1024], in_=PS[2][0:32, :]), w=[mmd, eva])
            evb = ctx.v(lambda e, S: e.tensor_copy(
                out=t32f[:, 1024:2048], in_=PS[3][0:32, :]), w=[mmd, evb])

            def e2w(e, S):
                for arm in e.switch_core_id(n=ARMN):
                    hh = int(arm) % 2
                    e.dma_start(out=s_xdbl[hh, 0:128, :],
                                in_=xdbla[:, 0:2048]).then_inc(S["e2w0"], 16)
                    e.wait_ge(S["e2w0"], W_E2W[0] + 16)
                    e.dma_start(out=s_xdbl[hh, 128:160, :],
                                in_=t32f[:, 0:2048]).then_inc(S["e2w0"], 16)
                return None
            W_E2W = [ctx.cnt["e2w0"]]
            t_e2w = ctx.emit("sync", e2w, w=[eva, evb], inc="e2w0", k=32)
            b2 = barrier(w=[t_e2w], pairs=True)

            def e2r(e, S):
                for arm in e.switch_core_id(n=ARMN):
                    hh = int(arm) % 2
                    e.dma_start(out=xdbla[:, 0:2048],
                                in_=s_xdbl[1 - hh, 0:128, :],
                                accum_op=OP.add).then_inc(S["e2r0"], 16)
                    e.wait_ge(S["e2r0"], W_E2R[0] + 16)
                    e.dma_start(out=t32f[:, 0:2048],
                                in_=s_xdbl[1 - hh, 128:160, :],
                                accum_op=OP.add).then_inc(S["e2r0"], 16)
                return None
            W_E2R = [ctx.cnt["e2r0"]]
            t_e2r = ctx.emit("gpsimd", e2r, w=[b2], inc="e2r0", k=32)
            cdt = ctx.v(lambda e, S: e.tensor_copy(
                out=dtinb[:, 0:2048], in_=xdbla[:, 0:2048]), w=[t_e2r])
            cbc = ctx.v(lambda e, S: e.tensor_copy(
                out=bc_bf[:, 0:2048], in_=t32f[:, 0:2048]), w=[t_e2r])

            if msub < 4:
                return None
            lwdt = ctx.dma("wdtl", lambda e, S: e.dma_start(
                out=wdtb_s[:, 0:2048], in_=d_wdt[:, :]),
                w=[eva, ctx.tok("v"), ctx.tok("s")])
            dt_parks = []
            sp_prev = {}
            sp_scr = [None]
            import os as _os
            for m in range(NDT):
                p = m % 2
                sp_toks = []
                for nch in range(4):
                    q = (m * 4 + nch) % 2
                    import os as _os3
                    _mcalt = _os3.environ.get("MC_ALT", "0") == "1"
                    mm = ctx.t(lambda e, S, m=m, nch=nch, q=q: e.matmul(
                        PS[q][:, 0:512],
                        (winb if _mcalt else wdtb_s)[:, m * 128:m * 128 + 128],
                        dtinb[:, nch * 512:nch * 512 + 512],
                        start=True, stop=True),
                        w=[lwdt, cdt] +
                        ([sp_prev[(m * 4 + nch - 2)]]
                         if (m * 4 + nch) >= 2 else []))
                    v0 = ctx.v(lambda e, S, m=m, q=q: e.tensor_scalar(
                        out=das[:, 0:512], in0=PS[q][:, 0:512],
                        scalar1=bdtt[:, m:m + 1], scalar2=None, op0=OP.add),
                        w=[mm, sp_scr[0]])
                    s1 = ctx.s(lambda e, S: e.activation(
                        out=das[:, 512:1024], in_=das[:, 0:512], func=AF.Abs,
                        bias=0.0, scale=1.0), w=[v0])
                    s2 = ctx.s(lambda e, S: e.activation(
                        out=das[:, 1024:1536], in_=das[:, 512:1024],
                        func=AF.Exp, bias=0.0, scale=-1.0), w=[s1])
                    s3 = ctx.s(lambda e, S: e.activation(
                        out=das[:, 512:1024], in_=das[:, 1024:1536],
                        func=AF.Ln, bias=1.0, scale=1.0), w=[s2])
                    v1 = ctx.v(lambda e, S: e.tensor_scalar(
                        out=das[:, 1536:2048], in0=das[:, 0:512],
                        scalar1=0.0, scalar2=None, op0=OP.max), w=[v0, s1])
                    sp = ctx.v(lambda e, S, m=m, nch=nch, p=p:
                               e.tensor_tensor(
                        out=dtstg[:, p * 2048 + nch * 512:
                                  p * 2048 + nch * 512 + 512],
                        in0=das[:, 512:1024], in1=das[:, 1536:2048],
                        op=OP.add),
                        w=[s3, v1] + ([dt_parks[m - 2]] if (m >= 2 and
                                                            nch == 0) else []))
                    sp_scr[0] = sp
                    sp_prev[(m * 4 + nch)] = sp
                    sp_toks.append(sp)
                import os as _os2
                if _os2.environ.get("MC_NOPARK", "0") != "1":
                    dt_parks.append(ctx.dma("dtp", lambda e, S, m=m, p=p:
                                            e.dma_start(
                        out=p_dt[:, m * 2048:(m + 1) * 2048],
                        in_=dtstg[:, p * 2048:p * 2048 + 2048]),
                        w=sp_toks))
                else:
                    dt_parks.append(sp_toks[-1])
            tok_dt = ctx.fence("dtp")

            if msub < 5:
                return None
            wo_evac = []
            po_parks = []
            for th in range(2):
                fv = ctx.tok("v")
                ft = ctx.tok("t")
                fs = ctx.tok("s")
                sdt = []
                sxc = []
                for m in range(NDT):
                    sdt.append(ctx.dma("sdt", lambda e, S, m=m, th=th:
                                       e.dma_start(
                        out=arb_b[:, m * T:(m + 1) * T],
                        in_=p_dt[:, m * 2048 + th * T:
                                 m * 2048 + th * T + T]),
                        w=[tok_dt, fv, ft, fs]))
                    sxc.append(ctx.dma("sxc", lambda e, S, m=m, th=th:
                                       e.dma_start(
                        out=arb_b[:, 16384 + m * T:16384 + (m + 1) * T],
                        in_=p_xc[:, m * 2048 + th * T:
                                 m * 2048 + th * T + T]),
                        w=tok_xc + [fv, ft, fs]))
                du_t = []
                for m in range(NDT):
                    du_t.append(ctx.v(lambda e, S, m=m: e.tensor_tensor(
                        out=arb_b[:, 32768 + m * T:32768 + (m + 1) * T],
                        in0=arb_b[:, m * T:(m + 1) * T],
                        in1=arb_b[:, 16384 + m * T:16384 + (m + 1) * T],
                        op=OP.mult), w=[sdt[m], sxc[m], fv]))
                ytoks = {}
                bc_prev = None
                das_last = {0: None, 1: None}
                for sp_i in range(8):
                    s0, s1 = 2 * sp_i, 2 * sp_i + 1
                    btoks = []
                    for js, sel in enumerate((s0, 16 + s0, s1, 16 + s1)):
                        mmb = None
                        for h2 in range(2):
                            fn = (lambda e, S, sel=sel, h2=h2, th=th:
                                  e.matmul(
                                PS[2][:, h2 * 512:h2 * 512 + 512],
                                eall[:, sel * 128:(sel + 1) * 128],
                                bc_bf[:, th * T + h2 * 512:
                                      th * T + h2 * 512 + 512],
                                start=True, stop=True))
                            wv = [cbc, ft] + ([bc_prev] if bc_prev else [])
                            mmb = ctx.t(fn, wv) if h2 == 1 else ctx.tq(fn, wv)
                        bc_prev = ctx.v(lambda e, S, js=js: e.tensor_copy(
                            out=bbcb[:, js * 1024:js * 1024 + 1024],
                            in_=PS[2][:, :]), w=[mmb])
                        btoks.append(bc_prev)
                    for m in range(NDT):
                        pps = []
                        sc_prev = None
                        for jj, s_ in enumerate((s0, s1)):
                            da = ctx.s(lambda e, S, m=m, s_=s_, jj=jj:
                                       e.activation(
                                out=das[:, jj * 1024:jj * 1024 + 1024],
                                in_=arb_b[:, m * T:(m + 1) * T], func=AF.Exp,
                                bias=0.0,
                                scale=acol[:, m * SS + s_:m * SS + s_ + 1]),
                                w=[sdt[m], das_last[jj], fv] +
                                ([ctx.tok("v")] if _os.environ.get("OLD_DA", "0") == "1" else []))
                            wk = ctx.v(lambda e, S, m=m, jj=jj:
                                       e.tensor_tensor(
                                out=wsc_[:, 0:1024],
                                in0=arb_b[:, 32768 + m * T:
                                          32768 + (m + 1) * T],
                                in1=bbcb[:, 2 * jj * 1024:
                                         2 * jj * 1024 + 1024],
                                op=OP.mult),
                                w=[du_t[m], btoks[2 * jj]] +
                                ([sc_prev] if sc_prev else []))
                            idx = m * SS + s_
                            sc_ = ctx.v(lambda e, S, jj=jj, idx=idx:
                                        e.tensor_tensor_scan(
                                out=hsc[:, 0:1024],
                                data0=das[:, jj * 1024:jj * 1024 + 1024],
                                data1=wsc_[:, 0:1024],
                                initial=states[:, idx:idx + 1],
                                op0=OP.mult, op1=OP.add),
                                w=[da, wk] + (pps[-1:] if pps else []))
                            das_last[jj] = sc_
                            sv = ctx.v(lambda e, S, idx=idx: e.tensor_copy(
                                out=states[:, idx:idx + 1],
                                in_=hsc[:, 1023:1024]), w=[sc_])
                            pp = ctx.v(lambda e, S, jj=jj: e.tensor_tensor(
                                out=(p0s if jj == 0 else p1s)[:, 0:1024],
                                in0=hsc[:, 0:1024],
                                in1=bbcb[:, (2 * jj + 1) * 1024:
                                         (2 * jj + 1) * 1024 + 1024],
                                op=OP.mult),
                                w=[sc_, sv, btoks[2 * jj + 1]])
                            pps.append(pp)
                            sc_prev = sc_
                        pr = ctx.v(lambda e, S: e.tensor_tensor(
                            out=p0s[:, 0:1024], in0=p0s[:, 0:1024],
                            in1=p1s[:, 0:1024], op=OP.add), w=pps)
                        if sp_i == 0:
                            ytoks[m] = ctx.v(lambda e, S, m=m: e.tensor_copy(
                                out=arb_b[:, 49152 + m * T:
                                          49152 + (m + 1) * T],
                                in_=p0s[:, 0:1024]), w=[pr, fv, ft])
                        else:
                            ytoks[m] = ctx.v(lambda e, S, m=m:
                                             e.tensor_tensor(
                                out=arb_b[:, 49152 + m * T:
                                          49152 + (m + 1) * T],
                                in0=arb_b[:, 49152 + m * T:
                                          49152 + (m + 1) * T],
                                in1=p0s[:, 0:1024], op=OP.add),
                                w=[pr, ytoks[m]])
                gat = {}
                for m in range(NDT):
                    p = m % 2
                    zl = ctx.dma("zl", lambda e, S, m=m, th=th, p=p:
                                 e.dma_start(
                        out=zst[:, p * 1024:p * 1024 + 1024],
                        in_=p_z[:, m * 2048 + th * T: m * 2048 + th * T + T]),
                        w=ctx.fence("zp") + [ytoks[m]] +
                        ([gat[m - 2]] if m >= 2 else []))
                    g1 = ctx.v(lambda e, S, m=m: e.scalar_tensor_tensor(
                        out=arb_b[:, 49152 + m * T:49152 + (m + 1) * T],
                        in0=arb_b[:, 16384 + m * T:16384 + (m + 1) * T],
                        scalar=ddt[:, m:m + 1],
                        in1=arb_b[:, 49152 + m * T:49152 + (m + 1) * T],
                        op0=OP.mult, op1=OP.add), w=[ytoks[m], ytoks[NDT - 1]])
                    gat[m] = ctx.v(lambda e, S, m=m, p=p: e.tensor_tensor(
                        out=arb_b[:, 49152 + m * T:49152 + (m + 1) * T],
                        in0=arb_b[:, 49152 + m * T:49152 + (m + 1) * T],
                        in1=zst[:, p * 1024:p * 1024 + 1024],
                        op=OP.mult), w=[g1, zl])
                wo_tok = {}
                for m in range(NF):
                    p = m % 2
                    pw = [gat[NDT - 1], gat[NDT - 2]] + \
                        ([wo_tok[m - 2]] if m >= 2 else []) + \
                        (wo_evac[-2:-1] if len(wo_evac) >= 2 else [])
                    wo_tok[m] = ctx.dma("wol", lambda e, S, m=m, p=p:
                                        e.dma_start(
                        out=woutb_s[:, p * 2048:p * 2048 + 2048],
                        in_=d_wout[:, m * 2048:(m + 1) * 2048]), w=pw)
                    dps = PS[p]
                    mm = None
                    for k in range(NDT):
                        for h2 in range(2):
                            fn = (lambda e, S, m=m, p=p, k=k, h2=h2, dps=dps:
                                  e.matmul(
                                dps[:, h2 * 512:h2 * 512 + 512],
                                woutb_s[:, p * 2048 + k * 128:
                                        p * 2048 + k * 128 + 128],
                                arb_b[:, 49152 + k * T + h2 * 512:
                                       49152 + k * T + h2 * 512 + 512],
                                start=(k == 0), stop=(k == NDT - 1)))
                            wv = [wo_tok[m], gat[NDT - 1]] + \
                                (wo_evac[-2:-1] if len(wo_evac) >= 2 else [])
                            mm = ctx.t(fn, wv) if (k == NDT - 1 and h2 == 1) \
                                else ctx.tq(fn, wv)
                    ev = ctx.v(lambda e, S, m=m, p=p, dps=dps: e.tensor_copy(
                        out=postg[:, p * 1024:p * 1024 + 1024],
                        in_=dps[:, :]),
                        w=[mm] + (po_parks[-2:-1] if len(po_parks) >= 2
                                  else []))
                    wo_evac.append(ev)
                    po_parks.append(ctx.dma(
                        "pop", lambda e, S, m=m, p=p, th=th: e.dma_start(
                            out=p_po[:, m * 2048 + th * T:
                                     m * 2048 + th * T + T],
                            in_=postg[:, p * 1024:
                                      p * 1024 + 1024]), w=[ev]))
            tok_po = ctx.fence("pop")

            if msub < 6:
                return None
            def e3w(e, S):
                for arm in e.switch_core_id(n=ARMN):
                    hh = int(arm) % 2
                    e.dma_start(out=s_po[hh, :, :],
                                in_=po_v[:, :, (1 - hh) * T:(2 - hh) * T]
                                ).then_inc(S["e3w0"], 16)
                return None
            t_e3w = ctx.emit("sync", e3w, w=tok_po, inc="e3w0", k=16)
            b3 = barrier(w=[t_e3w], pairs=True)
            if _os.environ.get("E3_NOR", "0") == "1":
                return None
            xlo_v = x_lo[:, :].rearrange("c (m t) -> c m t", m=8)
            xhi_v = x_hi[:, :].rearrange("c (m t) -> c m t", m=8)

            _e3rn = int(_os.environ.get("E3R_N", "6"))

            def e3r(e, S):
                for arm in e.switch_core_id(n=ARMN):
                    hh = int(arm) % 2
                    base = W_E3R[0]
                    todo = _e3rn
                    dmas = [
                        lambda: e.dma_start(out=x_lo[:, :],
                                            in_=s_po[1 - hh, :, 0:8 * T]),
                        lambda: e.dma_start(out=x_hi[:, :],
                                            in_=s_po[1 - hh, :, 8 * T:16 * T]),
                        lambda: e.dma_start(out=xlo_v[:, :, :],
                                            in_=po_v[:, 0:8,
                                                     hh * T:(hh + 1) * T],
                                            accum_op=OP.add),
                        lambda: e.dma_start(out=xhi_v[:, :, :],
                                            in_=po_v[:, 8:16,
                                                     hh * T:(hh + 1) * T],
                                            accum_op=OP.add),
                        lambda: e.dma_start(out=x_lo[:, :], in_=p_x[:, 0:8 * T],
                                            accum_op=OP.add,
                                            max_dma_last_dim=2048),
                        lambda: e.dma_start(out=x_hi[:, :],
                                            in_=p_x[:, 8 * T:16 * T],
                                            accum_op=OP.add,
                                            max_dma_last_dim=2048),
                    ]
                    for j in range(6):
                        if j < _e3rn:
                            dmas[j]().then_inc(S["e3r0"], 16)
                        else:
                            e.dma_start(out=x_lo[:, 0:64],
                                        in_=p_x[:, 0:64]).then_inc(
                                S["e3r0"], 16)
                        if j % 2 == 1:
                            e.wait_ge(S["e3r0"], base + 16 * (j + 1))
                return None
            W_E3R = [ctx.cnt["e3r0"]]
            tok_x2 = ctx.emit("gpsimd", e3r,
                              w=[b3, ctx.tok("v"), ctx.tok("s"),
                                 ctx.tok("t")], inc="e3r0", k=96)
            return tok_x2

        final_src = p_x
        tok_x1 = emit_ffn(p_x, tok_x_in)
        if stage >= 2:
            lds = []
            for k in range(NF):
                lds.append(ctx.dma("prkr", lambda e, S, k=k: e.dma_start(
                    out=xtile(k),
                    in_=p_x[:, k * T:(k + 1) * T]),
                    w=[tok_x1, ctx.tok("v"), ctx.tok("t"), ctx.tok("s")]))
            tok_x2 = emit_mamba(ctx.fence("prkr"))
            if tok_x2 is None:
                stage = 1
                final_src = p_x
            if stage >= 3:
                tok_x3 = emit_ffn(p_x3, tok_x2)
                final_src = p_x3
            else:
                for k in range(NF):
                    ctx.dma("prkw", lambda e, S, k=k: e.dma_start(
                        out=p_x3[:, k * T:(k + 1) * T],
                        in_=xtile(k)), w=[tok_x2])
                final_src = p_x3
        else:
            final_src = p_x

        otp = []
        base_w = (ctx.fence("prkw") + ctx.fence("e3r") +
                  [ctx.tok("v"), ctx.tok("t"), ctx.tok("s")])
        for tc in range(8):
            for fc in range(NF):
                k = tc * NF + fc
                ld = ctx.dma("outl", lambda e, S, tc=tc, fc=fc, k=k:
                             e.dma_start(
                    out=arb_f[:, (k % 4) * 128:(k % 4) * 128 + 128],
                    in_=final_src[:, fc * T + tc * 128:
                                  fc * T + tc * 128 + 128]),
                    w=base_w + ([otp[k - 4]] if k >= 4 else []))
                tt = ctx.t(lambda e, S, k=k: e.transpose(
                    out=PS[k % 2][:, 0:128],
                    in_=arb_f[:, (k % 4) * 128:(k % 4) * 128 + 128],
                    identity=id128[:, :]),
                    w=[ld] + ([otp[k - 2]] if k >= 2 else []))
                otp.append(ctx.v(lambda e, S, tc=tc, fc=fc, k=k: e.tensor_copy(
                    out=arb_f[:, 2048 + tc * 2048 + fc * 128:
                           2048 + tc * 2048 + fc * 128 + 128],
                    in_=PS[k % 2][:, 0:128]), w=[tt]))
        for tc in range(8):
            ctx.dma("outs", lambda e, S, tc=tc: e.dma_start(
                out=d_out[tc * 128:(tc + 1) * 128, :],
                in_=arb_f[:, 2048 + tc * 2048: 2048 + (tc + 1) * 2048]),
                w=[otp[tc * NF + NF - 1]])
        ctx.emit("sync", lambda e, S: None, w=ctx.fence("outs"))

        for _sn in ctx.semnames:
            sems[_sn] = _stk.enter_context(nc.semaphore(_sn))
        block = _stk.enter_context(nc.Block())

        def run(e, eng):
            for fn, wl, inc in ctx.q[eng]:
                for sname, vv in wl:
                    e.wait_ge(sems[sname], vv)
                inst = fn(e, sems)
                if inc and inst is not None:
                    inst.then_inc(sems[inc[0]], inc[1])

        @block.sync
        def _(e):
            run(e, "sync")

        @block.tensor
        def _(e):
            run(e, "tensor")

        @block.vector
        def _(e):
            run(e, "vector")

        @block.scalar
        def _(e):
            run(e, "scalar")

        @block.gpsimd
        def _(e):
            run(e, "gpsimd")

    return nc



def _lhsT_flat(w):
    O, I_ = w.shape
    nm, nk = O // 128, I_ // 128
    return np.ascontiguousarray(
        w.reshape(nm, 128, nk, 128).transpose(3, 0, 2, 1).reshape(128, -1))


def _lhsT_flat_in(w):
    I_, O = w.shape
    nk, nm = I_ // 128, O // 128
    return np.ascontiguousarray(
        w.reshape(nk, 128, nm, 128).transpose(1, 2, 0, 3).reshape(128, -1))


def make_in_maps(inputs):
    f32 = np.float32
    x = np.asarray(inputs["L0_attn_x"], f32)
    wg = np.asarray(inputs["L0_ffn_w_gate"], f32)
    wu = np.asarray(inputs["L0_ffn_w_up"], f32)
    wd = np.asarray(inputs["L0_ffn_w_down"], f32)
    win = np.asarray(inputs["L0_attn_W_in"], f32)
    wx = np.asarray(inputs["L0_attn_W_x"], f32)
    wdt = np.asarray(inputs["L0_attn_W_dt"], f32)
    wout = np.asarray(inputs["L0_attn_W_out"], f32)
    cw = np.asarray(inputs["L0_attn_conv_w"], f32)
    cb = np.asarray(inputs["L0_attn_conv_b"], f32)
    al = np.asarray(inputs["L0_attn_A_log"], f32)
    dd = np.asarray(inputs["L0_attn_D"], f32)
    bdt = np.asarray(inputs["L0_attn_b_dt"], f32)

    fg, fu, fd = _lhsT_flat(wg), _lhsT_flat(wu), _lhsT_flat(wd)

    h = np.array([[1.0]], f32)
    while h.shape[0] < 128:
        h = np.block([[h, h], [h, -h]])
    ident = np.eye(128, dtype=f32)
    eallm = np.zeros((32, 32 * 128), bfdt)
    for j in range(32):
        eallm[j, j * 128:(j + 1) * 128] = 1.0

    def tile_col(v):
        return np.ascontiguousarray(v.reshape(NDT, 128).T)

    win_b, wout_b, wx_b = {}, {}, {}
    for g in range(2):
        cols = np.concatenate([np.arange(g * DH, (g + 1) * DH),
                               np.arange(D + g * DH, D + (g + 1) * DH)])
        win_b[g] = _lhsT_flat_in(np.ascontiguousarray(win[:, cols]))
        wout_b[g] = _lhsT_flat_in(
            np.ascontiguousarray(wout[g * DH:(g + 1) * DH, :]))
        wx_b[g] = np.ascontiguousarray(
            wx[g * DH:(g + 1) * DH, :].reshape(NDT, 128, 160)
            .transpose(1, 0, 2).reshape(128, -1))

    maps = []
    for c in range(NC):
        beta, rho = c // 2, c % 2
        g = rho
        sl = np.s_[c * 16384:(c + 1) * 16384]
        p1 = np.ascontiguousarray(
            np.concatenate([fg[:, sl], fu[:, sl], fd[:, sl]], axis=1))
        chs = slice(g * DH, (g + 1) * DH)
        cwv = cw[chs, 0, :]
        cwt_ = np.zeros((128, 3 * NDT), f32)
        for m in range(NDT):
            for j in range(3):
                cwt_[:, 3 * m + j] = cwv[m * 128:(m + 1) * 128, j]
        alv = al[chs, :]
        alt_ = np.zeros((128, SS * NDT), f32)
        for m in range(NDT):
            alt_[:, m * SS:(m + 1) * SS] = alv[m * 128:(m + 1) * 128, :]
        maps.append({
            "x0": np.ascontiguousarray(x[beta, rho * T:(rho + 1) * T, :]),
            "wgb": fg, "wub": fu, "wdb": fd,
            "winb": win_b[g], "woutb": wout_b[g], "wxb": wx_b[g],
            "wdtb": np.ascontiguousarray(wdt[:, chs]),
            "cwt": cwt_, "cbt": tile_col(cb[chs]), "ddt": tile_col(dd[chs]),
            "bdtt": tile_col(bdt[chs]), "alt": alt_,
            "ident": ident, "h128": h.astype(bfdt),
            "onesc": np.ones((128, 2), f32),
            "onesb": np.ones((128, 2), bfdt),
            "onerowf": np.ones((1, 128), f32),
            "eall": eallm, "p1chunk": p1,
        })
    return maps


def kernel(**inputs):
    nc = build_program(STAGE)
    in_maps = make_in_maps(inputs)
    res = run_bass_kernel_spmd(nc, in_maps, core_ids=list(range(NC)))
    out = np.empty((B, L, H), np.float32)
    for c in range(NC):
        beta, rho = c // 2, c % 2
        out[beta, rho * T:(rho + 1) * T, :] = np.asarray(
            res.results[c]["xout"])
    return out
